# version 21
# baseline (speedup 1.0000x reference)
import numpy as np
import ml_dtypes
import concourse.bass as bass
import concourse.mybir as mybir
from concourse.bass_utils import run_bass_kernel_spmd
from contextlib import ExitStack

F32 = mybir.dt.float32
BF16 = mybir.dt.bfloat16
U8 = mybir.dt.uint8
AF = mybir.ActivationFunctionType
AX = mybir.AxisListType
OP = mybir.AluOpType

NCORES = 8
T = 1024
TT = 8
D = 2048
FT = 16
HKV = 4
NU = 3
FDIM = 768
FK = 6
E = 16
FE = 1024
FSH = 2048
EPS_A = 1e-8
EPS_LN = 1e-5
L2EPS = 1e-12
BF = ml_dtypes.bfloat16


class Sem:
    def __init__(self, pl, name):
        self.name = name
        self.h = pl.stack.enter_context(pl.nc.semaphore(name))
        self.n = 0


class Plan:
    ENGS = ("tensor", "vector", "scalar", "gpsimd", "sync")

    def __init__(self, nc, stack):
        self.nc = nc
        self.stack = stack
        self.ops = {k: [] for k in self.ENGS}
        self.waited = {k: {} for k in self.ENGS}

    def sem(self, name):
        return Sem(self, name)

    def wait(self, eng, sem, val):
        if sem is None or val is None or val <= 0:
            return
        if self.waited[eng].get(sem.name, 0) >= val:
            return
        self.waited[eng][sem.name] = val
        self.ops[eng].append(lambda Eng, h=sem.h, v=val: Eng.wait_ge(h, v))

    def op(self, eng, build, sem=None, k=1):
        if sem is None:
            self.ops[eng].append(build)
            return None
        sem.n += k
        val = sem.n
        self.ops[eng].append(lambda Eng, b=build, h=sem.h, kk=k: b(Eng).then_inc(h, kk))
        return val

    def sop(self, eng, build, sem):
        self.wait(eng, sem, sem.n)
        return self.op(eng, build, sem=sem)

    def sdma(self, eng, build, sem, k=16):
        self.wait(eng, sem, sem.n)
        return self.op(eng, build, sem=sem, k=k)


def bMM(o, l, r, s, p):
    return lambda E: E.matmul(o, l, r, start=s, stop=p)


def bTR(o, i, ident):
    return lambda E: E.transpose(o, i, ident)


def bDMA(o, i, **kw):
    return lambda E: E.dma_start(out=o, in_=i, **kw)


def bTT(o, a, b, opk):
    return lambda E: E.tensor_tensor(o, a, b, opk)


def bTS(o, a, s1, s2, o0, o1):
    return lambda E: E.tensor_scalar(o, a, s1, s2, o0, o1)


def bTSS(o, a, s, opk):
    return lambda E: E.tensor_single_scalar(o, a, s, opk)


def bCOPY(o, a):
    return lambda E: E.tensor_copy(o, a)


def bRED(o, a, opk):
    return lambda E: E.tensor_reduce(o, a, axis=AX.X, op=opk)


def bACT(o, a, f, bias=0.0, scale=1.0, accum=None):
    return lambda E: E.activation(o, a, f, bias=bias, scale=scale, accum_out=accum)


def bRECIP(o, a):
    return lambda E: E.reciprocal(o, a)


def bMEMSET(a, c):
    return lambda E: E.memset(a, c)


def bSTT(o, a, s, b, o0, o1):
    return lambda E: E.scalar_tensor_tensor(o, a, s, b, o0, o1)


class Bank:
    def __init__(self, handle):
        self.h = handle
        self.free_sem = None
        self.free_val = 0


DEBUG = False
_LAST_RES = [None]


def build_program():
    nc = bass.Bass("TRN2", target_bir_lowering=False, debug=False)

    dX = nc.dram_tensor("x", [T, D], F32, kind="ExternalInput").ap()
    dWkT = nc.dram_tensor("wkT", [D, 512], F32, kind="ExternalInput").ap()
    dWvT = nc.dram_tensor("wvT", [D, 512], F32, kind="ExternalInput").ap()
    dWqT = nc.dram_tensor("wqT", [D, D], F32, kind="ExternalInput").ap()
    dWgT = nc.dram_tensor("wgT", [D, D], F32, kind="ExternalInput").ap()
    dWmbT = nc.dram_tensor("wmbT", [D, D], F32, kind="ExternalInput").ap()
    dBk = nc.dram_tensor("bk", [512], F32, kind="ExternalInput").ap()
    dBv = nc.dram_tensor("bv", [512], F32, kind="ExternalInput").ap()
    dBq = nc.dram_tensor("bq", [D], F32, kind="ExternalInput").ap()
    dBg = nc.dram_tensor("bg", [D], F32, kind="ExternalInput").ap()
    dBmb = nc.dram_tensor("bmb", [D], F32, kind="ExternalInput").ap()
    dMemKH = nc.dram_tensor("mem_kh", [128, 12288], F32, kind="ExternalInput").ap()
    dMemHC = nc.dram_tensor("mem_hc", [128, 12288], F32, kind="ExternalInput").ap()
    dNormPC = nc.dram_tensor("norm_pc", [128, 96], F32, kind="ExternalInput").ap()
    dRouT = nc.dram_tensor("routerT", [D, 16], F32, kind="ExternalInput").ap()
    dGateW = nc.dram_tensor("gate_w", [E, D, FE], F32, kind="ExternalInput").ap()
    dUpW = nc.dram_tensor("up_w", [E, D, FE], F32, kind="ExternalInput").ap()
    dDownW = nc.dram_tensor("down_w", [E, FE, D], F32, kind="ExternalInput").ap()
    dSgw = nc.dram_tensor("sgw", [D, FSH], F32, kind="ExternalInput").ap()
    dSuw = nc.dram_tensor("suw", [D, FSH], F32, kind="ExternalInput").ap()
    dSdw = nc.dram_tensor("sdw", [FSH, D], F32, kind="ExternalInput").ap()
    dIdent = nc.dram_tensor("ident", [128, 128], BF16, kind="ExternalInput").ap()
    dIdentF = nc.dram_tensor("identf", [128, 128], F32, kind="ExternalInput").ap()
    dIota = nc.dram_tensor("iota16", [128, 16], F32, kind="ExternalInput").ap()
    dOut = nc.dram_tensor("out", [T, D], F32, kind="ExternalOutput").ap()

    dCumLT = nc.dram_tensor("cumlt", [128, 128], BF16, kind="ExternalInput").ap()
    dU8 = nc.dram_tensor("u8t", [8, 8], BF16, kind="ExternalInput").ap()
    dEc = nc.dram_tensor("ec256", [128, 128], F32, kind="ExternalInput").ap()
    dTokAB = nc.dram_tensor("tokab", [128, 16], mybir.dt.int32, kind="ExternalInput").ap()

    dAcc = nc.dram_tensor("acc_d", [128, 16384], F32, kind="Internal")
    dDummy = nc.dram_tensor("dummy_d", [128, 128], BF16, kind="Internal")
    dSd = nc.dram_tensor("s_d", [128, 16384], BF16, kind="Internal")
    dXs = nc.dram_tensor("xs_d", [4096, 2048], BF16, kind="Internal")
    dTok = nc.dram_tensor("tok_d", [4096, 1], mybir.dt.int32, kind="Internal")
    dWsl = nc.dram_tensor("wsl_d", [4096, 1], F32, kind="Internal")
    dBlob = nc.dram_tensor("blob_d", [2048, 2048], BF16, kind="Internal")
    dCCmi = nc.dram_tensor("cc_mem_in", [128, 12288], F32, kind="Internal")
    dCCmo = nc.dram_tensor("cc_mem_out", [128, 12288], F32, kind="Internal")
    dCCni = nc.dram_tensor("cc_norm_in", [128, 96], F32, kind="Internal")
    dCCno = nc.dram_tensor("cc_norm_out", [128, 96], F32, kind="Internal")

    stack = ExitStack()
    with stack:
        al = nc.alloc_sbuf_tensor

        ident = al("ident_sb", [128, 128], BF16).ap()
        identf = al("identf_sb", [128, 128], F32).ap()
        onesb = al("ones_bf", [1, 128], BF16).ap()
        onesf = al("ones_f32", [1, 128], F32).ap()
        iota16 = al("iota_sb", [128, 16], F32).ap()
        bk_row = al("bk_row", [1, 512], BF16).ap()
        bv_row = al("bv_row", [1, 512], BF16).ap()
        bq_row = al("bq_row", [1, D], BF16).ap()
        bg_row = al("bg_row", [1, D], BF16).ap()
        bmb_row = al("bmb_row", [1, D], BF16).ap()
        norm_bf = al("norm_bf", [128, 96], BF16).ap()
        norm_f = al("norm_f", [128, 96], F32).ap()
        rouT_sb = al("rouT_sb", [128, 256], F32).ap()
        sx = al("sx", [128, 8], F32).ap()
        sxx = al("sxx", [128, 8], F32).ap()
        stat = al("stat", [128, 48], F32).ap()
        stat2 = al("stat2", [128, 96], F32).ap()
        ln_rinv = al("ln_rinv", [128, 8], F32).ap()
        ln_nmr = al("ln_nmr", [128, 8], F32).ap()
        m1_all = al("m1_all", [128, 128], F32).ap()
        m2_all = al("m2_all", [128, 128], F32).ap()
        mask_tok = al("mask_tok", [128, 128], F32).ap()
        wcols = al("wcols", [128, 32], F32).ap()
        ssc = al("ssc", [128, 8], F32).ap()
        rinvc = al("rinvc", [128, 8], F32).ap()
        mksqr = al("mksqr", [128, 8], F32).ap()
        den4 = al("den4", [128, 32], F32).ap()
        drec4 = al("drec4", [128, 32], F32).ap()
        coef4 = al("coef4", [128, 32], BF16).ap()
        denac = al("denac", [128, 8], F32).ap()
        darec = al("darec", [128, 8], F32).ap()
        nn_bf = al("nn_bf", [128, 96], BF16).ap()
        ccn_sb = al("ccn_sb", [128, 96], F32).ap()
        epsln_c = al("epsln_c", [128, 1], F32).ap()
        I32 = mybir.dt.int32
        u8_sb = al("u8_sb", [8, 8], BF16).ap()
        tokab_sb = al("tokab_sb", [128, 16], I32).ap()
        onescol_bf = al("onescol_bf", [128, 1], BF16).ap()
        tots_f = al("tots_f", [16, 8], F32).ap()
        totsT_bf = al("totsT_bf", [8, 16], BF16).ap()
        texE_f = al("texE_f", [16, 8], F32).ap()
        texT_f = al("texT_f", [8, 16], F32).ap()
        texrow = al("texrow", [1, 128], F32).ap()
        destA_f = al("destA_f", [128, 8], F32).ap()
        destB_f = al("destB_f", [128, 8], F32).ap()
        destA_i = al("destA_i", [128, 8], I32).ap()
        destB_i = al("destB_i", [128, 8], I32).ap()
        bigi32 = al("bigi32", [128, 32], I32).ap()
        tok_col = al("tok_col", [128, 32], I32).ap()
        wsl_col = al("wsl_col", [128, 32], F32).ap()
        pv_scr = al("pv_scr", [128, 128], F32).ap()
        mv_scr = al("mv_scr", [128, 128], F32).ap()
        mh_scr = al("mh_scr", [128, 128], F32).ap()
        sg_scr = al("sg_scr", [128, 128], F32).ap()

        xn_fm = al("xn_fm", [128, FT * T], BF16).ap()
        RX = al("RX", [128, 65536], U8)
        RS = al("RS", [128, 24576], U8)
        RW = al("RW", [128, 32768], U8)
        RD = al("RD", [128, 16384], U8)
        RH = al("RH", [128, 16384], U8)

        def carve(region, off, n, dt):
            size = 4 if dt == F32 else 2
            return region.ap()[:, off:off + n * size].bitcast(dt)

        x_all = carve(RX, 0, 16384, F32)
        xn32_scr = carve(RS, 0, 2048, F32)
        xnbf_all = [carve(RX, t * 8192 + 4096, 2048, BF16) for t in range(TT)]
        xnfm32_t = carve(RS, 12288, 2048, F32)
        dump0 = carve(RS, 20480, 2048, BF16)
        mem_kh = carve(RX, 0, 12288, BF16)
        mk_tok = carve(RX, 24576, 6144, BF16)
        mk_fm = carve(RX, 36864, 6144, BF16)
        v4_bf = carve(RX, 49152, 4096, BF16)
        mb4_bf = carve(RX, 57344, 4096, BF16)
        wmv4 = carve(RH, 0, 4096, BF16)
        k4_bf = carve(RH, 8192, 4096, BF16)
        phi_bf = carve(RS, 0, 6144, BF16)
        a_scr = carve(RS, 12288, 256, F32)
        dump1 = carve(RS, 13568, 768, BF16)
        a1_all = carve(RD, 4096, 2048, BF16)
        stg_m = carve(RD, 0, 768, F32)
        stg_n = carve(RD, 3072, 6, F32)
        ccum_f = carve(RD, 4096, 128, F32)
        rank_f = carve(RD, 4608, 128, F32)
        oh_bf = carve(RD, 5120, 128, BF16)
        ec_sb = carve(RD, 5632, 128, F32)
        cumlt_sb = carve(RD, 6144, 128, BF16)
        logit = carve(RD, 6656, 128, F32)
        zrob = carve(RD, 8192, 2048, BF16)
        gate_buf = carve(RW, 0, 8192, BF16)
        up_buf = carve(RW, 16384, 8192, BF16)
        dn_buf = [carve(RD, 0, 4096, BF16), carve(RD, 8192, 4096, BF16)]
        acst = [carve(RS, i * 2048, 512, F32) for i in range(4)]
        mask_em = carve(RS, 8192, 1024, F32)[0:16, :]
        mrow1 = carve(RS, 12288, 1024, F32)[0:1, :]
        mb_sb = carve(RS, 16384, 1024, F32)
        sil_scr = carve(RS, 20480, 512, BF16)
        h1_scr = carve(RS, 21504, 512, BF16)
        phi4 = [carve(RS, 0, 6144, BF16), carve(RX, 0, 6144, BF16)]
        mq_fm2 = [carve(RX, 12288, 6144, BF16), carve(RX, 32768, 6144, BF16)]
        q4_bf = carve(RX, 24576, 4096, BF16)
        s_h = carve(RX, 45056, 1024, BF16)
        wq_buf = carve(RW, 0, 8192, BF16)
        wg_buf = carve(RW, 16384, 8192, BF16)
        ccm_sb = carve(RD, 0, 768, F32)
        newm_bf = carve(RD, 6144, 768, BF16)
        memh_sb = carve(RD, 12288, 768, F32)
        dump4 = carve(RD, 3072, 768, BF16)
        a4_all = carve(RD, 7680, 2048, BF16)
        g4_bf = carve(RS, 13312, 4096, BF16)
        xnt_h = carve(RS, 21504, 1024, BF16)
        acc_t = [carve(RX, 0, 2048, F32), carve(RX, 8192, 2048, F32)]
        x5_t = [carve(RX, 16384, 2048, F32), carve(RX, 24576, 2048, F32)]
        silu5 = [carve(RS, 8192, 2048, BF16), carve(RS, 12288, 2048, BF16)]
        t2_scr = carve(RS, 0, 2048, F32)
        out_st = [carve(RD, 0, 2048, F32), carve(RD, 8192, 2048, F32)]

        pb = [Bank(nc.alloc_psum_tensor(f"pb{i}", [128, 512], F32).ap()) for i in range(5)]
        pbS = Bank(nc.alloc_psum_tensor("pbS", [128, 512], F32).ap())
        pbD = Bank(nc.alloc_psum_tensor("pbD", [128, 512], F32).ap())
        ptr = Bank(nc.alloc_psum_tensor("ptr", [128, 1024], BF16).ap())

        pl = Plan(nc, stack)
        PEs = pl.sem("PEs")
        VEs = pl.sem("VEs")
        SEs = pl.sem("SEs")
        dwC = pl.sem("dwC")
        dxs = pl.sem("dxs")
        csem = pl.sem("csem")
        bsem = {}

        def get_bsem(key):
            if key not in bsem:
                bsem[key] = pl.sem("b_" + key)
            return bsem[key]

        rr = [0]
        regbox = {}

        def mk_reg(key, val):
            def f(Eng, k=key, v=val):
                regbox[k] = Eng.to_reg(v)
            return f

        def next_bank():
            b = pb[rr[0] % len(pb)]
            rr[0] += 1
            return b

        def use_bank(b):
            if b.free_sem is not None:
                pl.wait("tensor", b.free_sem, b.free_val)

        def evac(b, build, eng="vector", pe_val=None):
            s = VEs if eng == "vector" else SEs
            pl.wait(eng, PEs, pe_val)
            v = pl.op(eng, build, sem=s)
            b.free_sem, b.free_val = s, v
            return v

        consts_x_box = [0]

        def transpose_batch(srcs, dst, idn=None, extra_wait=None):
            use_bank(ptr)
            pl.wait("tensor", dxs, consts_x_box[0])
            if extra_wait is not None:
                pl.wait("tensor", extra_wait[0], extra_wait[1])
            pe_val = None
            for i, s in enumerate(srcs):
                o = ptr.h[:, i * 128:(i + 1) * 128]
                pe_val = pl.op("tensor", bTR(o, s, idn if idn is not None else ident),
                               sem=PEs)
            pl.wait("vector", PEs, pe_val)
            v = pl.op("vector", bCOPY(dst, ptr.h[:, 0:len(srcs) * 128]), sem=VEs)
            ptr.free_sem, ptr.free_val = VEs, v
            return v

        def emit_dpfp(src_fn, a_all, mk_dst, dump_ap, with_mksq, phi_ap=None):
            phi = phi_ap if phi_ap is not None else phi_bf
            if mk_dst is None:
                mk_dst = phi
            pl.wait("vector", VEs, VEs.n)
            pl.wait("vector", SEs, SEs.n)
            for t in range(TT):
                ab = t * 256
                pl.op("vector", bTSS(a_all[:, ab:ab + 128], src_fn(t), 0.0, OP.max),
                      sem=VEs)
                pl.op("vector", bTS(a_all[:, ab + 128:ab + 256], src_fn(t), -1.0, 0.0,
                                    OP.mult, OP.max), sem=VEs)
            pl.wait("vector", VEs, VEs.n)
            for t in range(TT):
                base = t * FDIM
                ab = t * 256
                for j in range(1, NU + 1):
                    o0 = base + (j - 1) * 256
                    pl.op("vector", bTT(phi[:, o0:o0 + j], a_all[:, ab:ab + j],
                                        a_all[:, ab + 256 - j:ab + 256], OP.mult),
                          sem=VEs)
                    pl.op("vector", bTT(phi[:, o0 + j:o0 + 256],
                                        a_all[:, ab + j:ab + 256],
                                        a_all[:, ab:ab + 256 - j], OP.mult), sem=VEs)
            phi_v = VEs.n
            pl.wait("scalar", VEs, phi_v)
            for t in range(TT):
                pl.sop("scalar", bACT(dump_ap, phi[:, t * FDIM:(t + 1) * FDIM],
                                      AF.Square, accum=ssc[:, t:t + 1]), SEs)
            pl.wait("scalar", SEs, SEs.n)
            sq_v = pl.op("scalar", bACT(stat[:, 0:8], ssc[:, 0:8], AF.Sqrt), sem=SEs)
            pl.wait("vector", SEs, sq_v)
            pl.sop("vector", bTSS(stat[:, 8:16], stat[:, 0:8], L2EPS, OP.max), VEs)
            pl.sop("vector", bRECIP(rinvc[:, 0:8], stat[:, 8:16]), VEs)
            if with_mksq:
                pl.sop("vector", bTT(stat[:, 16:24], rinvc[:, 0:8], rinvc[:, 0:8],
                                     OP.mult), VEs)
                pl.sop("vector", bTT(stat[:, 24:32], ssc[:, 0:8], stat[:, 16:24],
                                     OP.mult), VEs)
                pl.sop("vector", bRECIP(mksqr[:, 0:8], stat[:, 24:32]), VEs)
            pl.wait("vector", VEs, VEs.n)
            for t in range(TT):
                pl.op("vector", bTSS(mk_dst[:, t * FDIM:(t + 1) * FDIM],
                                     phi[:, t * FDIM:(t + 1) * FDIM],
                                     rinvc[:, t:t + 1], OP.mult), sem=VEs)
            return VEs.n

        def phase_barrier():
            for eng in Plan.ENGS:
                pl.wait(eng, PEs, PEs.n)
                pl.wait(eng, VEs, VEs.n)
                pl.wait(eng, SEs, SEs.n)

        wstate = {}

        def wfill(key, dst, src):
            st = wstate.get(key)
            if st is not None:
                pl.wait("gpsimd", st[0], st[1])
            s = get_bsem(key)
            v = pl.sdma("gpsimd", bDMA(dst, src), sem=s, k=16)
            return s, v

        def wused(key, val):
            wstate[key] = (PEs, val)

        def colsrc(w, chunk, cols=512):
            return w[:, chunk * cols:(chunk + 1) * cols].rearrange(
                "(kt p) c -> p kt c", p=128)

        def bufview(buf, cols=512):
            return buf.rearrange("p (kt c) -> p kt c", c=cols)

        pl.sdma("sync", bDMA(ident[:], dIdent), sem=dxs, k=16)
        pl.sdma("sync", bDMA(identf[:], dIdentF), sem=dxs, k=16)
        pl.sdma("sync", bDMA(iota16[:], dIota), sem=dxs, k=16)
        pl.sdma("sync", bDMA(rouT_sb.rearrange("p (kt e) -> p kt e", e=16),
                           dRouT.rearrange("(kt p) e -> p kt e", p=128)), sem=dxs, k=16)
        pl.sdma("sync", bDMA(norm_f[:], dNormPC), sem=dxs, k=16)
        pl.sdma("sync", bDMA(cumlt_sb[:], dCumLT), sem=dxs, k=16)
        pl.sdma("sync", bDMA(u8_sb[:], dU8), sem=dxs, k=16)
        pl.sdma("sync", bDMA(ec_sb[:], dEc), sem=dxs, k=16)
        pl.sdma("sync", bDMA(tokab_sb[:], dTokAB), sem=dxs, k=16)
        consts_x = dxs.n
        consts_x_box[0] = consts_x
        pl.sdma("gpsimd", bDMA(bk_row[0:1, :], dBk.rearrange("(o n) -> o n", o=1)), sem=dwC, k=16)
        pl.sdma("gpsimd", bDMA(bv_row[0:1, :], dBv.rearrange("(o n) -> o n", o=1)), sem=dwC, k=16)
        pl.sdma("gpsimd", bDMA(bq_row[0:1, :], dBq.rearrange("(o n) -> o n", o=1)), sem=dwC, k=16)
        pl.sdma("gpsimd", bDMA(bg_row[0:1, :], dBg.rearrange("(o n) -> o n", o=1)), sem=dwC, k=16)
        pl.sdma("gpsimd", bDMA(bmb_row[0:1, :], dBmb.rearrange("(o n) -> o n", o=1)), sem=dwC, k=16)
        pl.sdma("gpsimd", bDMA(norm_bf[:], dNormPC), sem=dwC, k=16)
        consts_w = dwC.n
        pl.op("vector", bMEMSET(onesb[:], 1.0), sem=VEs)
        pl.op("vector", bMEMSET(epsln_c[:], EPS_LN), sem=VEs)
        pl.op("vector", bMEMSET(onescol_bf[:], 1.0), sem=VEs)
        pl.op("vector", bMEMSET(bigi32[:], 8191), sem=VEs)
        pl.op("vector", bMEMSET(zrob[:], 0.0), sem=VEs)
        ones_v = pl.op("vector", bMEMSET(onesf[:], 1.0), sem=VEs)
        initS = get_bsem("init")
        pl.wait("gpsimd", VEs, VEs.n)
        pl.op("gpsimd", bDMA(dTok.ap().rearrange("(p c) o -> p (c o)", p=128),
                             bigi32[:]), sem=initS, k=16)
        for bchunk in range(8):
            pl.sdma("gpsimd", bDMA(
                dBlob.ap().rearrange("(b p) c -> b p c", p=128)[2 * bchunk],
                zrob[:]), sem=initS, k=16)
            pl.sdma("gpsimd", bDMA(
                dBlob.ap().rearrange("(b p) c -> b p c", p=128)[2 * bchunk + 1],
                zrob[:]), sem=initS, k=16)
        for bchunk in range(32):
            pl.sdma("gpsimd", bDMA(
                dXs.ap().rearrange("(b p) c -> b p c", p=128)[bchunk],
                zrob[:]), sem=initS, k=16)
        zrof = carve(RD, 8192, 1024, F32)
        pl.sdma("gpsimd", bDMA(dWsl.ap().rearrange("(p c) o -> p (c o)", p=128),
                             zrof[:, 0:32]), sem=initS, k=16)
        wkS, wkV = wfill("rw0", bufview(carve(RW, 0, 8192, BF16)),
                         dWkT.rearrange("(kt p) c -> p kt c", p=128))
        wvS, wvV = wfill("rw1", bufview(carve(RW, 16384, 8192, BF16)),
                         dWvT.rearrange("(kt p) c -> p kt c", p=128))
        wk_buf = carve(RW, 0, 8192, BF16)
        wv_buf = carve(RW, 16384, 8192, BF16)

        pl.sdma("sync", bDMA(x_all.rearrange("p (t c) -> p t c", c=D),
                           dX.rearrange("(t p) c -> p t c", p=128)), sem=dxs, k=16)
        xload = dxs.n
        consts_x_box[0] = xload

        pl.wait("scalar", dxs, xload)
        for t in range(TT):
            xt = x_all[:, t * D:(t + 1) * D]
            pl.sop("scalar", bACT(dump0[:], xt, AF.Copy, accum=sx[:, t:t + 1]), SEs)
            pl.sop("scalar", bACT(dump0[:], xt, AF.Square, accum=sxx[:, t:t + 1]), SEs)
        stats_done = SEs.n
        pl.wait("vector", SEs, stats_done)
        pl.sop("vector", bTSS(stat[:, 0:8], sx[:, 0:8], 1.0 / D, OP.mult), VEs)
        pl.sop("vector", bTSS(stat[:, 8:16], sxx[:, 0:8], 1.0 / D, OP.mult), VEs)
        pl.sop("vector", bTT(stat[:, 16:24], stat[:, 0:8], stat[:, 0:8], OP.mult), VEs)
        var_v = pl.sop("vector", bTT(stat[:, 24:32], stat[:, 8:16], stat[:, 16:24],
                                     OP.subtract), VEs)
        pl.wait("scalar", VEs, var_v)
        pl.wait("scalar", VEs, ones_v)
        sq_v = pl.op("scalar", bACT(stat[:, 32:40], stat[:, 24:32], AF.Sqrt,
                                    bias=epsln_c[:, 0:1]), sem=SEs)
        pl.wait("vector", SEs, sq_v)
        pl.sop("vector", bRECIP(ln_rinv[:, 0:8], stat[:, 32:40]), VEs)
        ln_v = pl.sop("vector", bSTT(ln_nmr[:, 0:8], stat[:, 0:8], -1.0, ln_rinv[:, 0:8],
                                     OP.mult, OP.mult), VEs)

        router_pe = None
        for t in range(TT):
            xt = x_all[:, t * D:(t + 1) * D]
            pl.wait("vector", VEs, ln_v)
            pl.wait("vector", PEs, PEs.n)
            s_v = pl.sop("vector", bTS(xn32_scr[:], xt, stat[:, t:t + 1],
                                       ln_rinv[:, t:t + 1], OP.subtract, OP.mult), VEs)
            c_v = pl.sop("vector", bCOPY(xnbf_all[t][:], xn32_scr[:]), VEs)
            for half in range(2):
                srcs = [xnbf_all[t][:, (half * 8 + j) * 128:(half * 8 + j + 1) * 128]
                        for j in range(8)]
                dst = xn_fm.rearrange("p (f q) -> p f q", q=T)[
                    :, half * 8:(half + 1) * 8, t * 128:(t + 1) * 128]
                transpose_batch(srcs, dst, extra_wait=(VEs, c_v))
            for quad in range(4):
                b = next_bank()
                use_bank(b)
                pl.wait("tensor", VEs, s_v)
                pe_val = None
                for j in range(4):
                    jj = quad * 4 + j
                    pe_val = pl.op("tensor", bTR(b.h[:, j * 128:(j + 1) * 128],
                                                 xn32_scr[:, jj * 128:(jj + 1) * 128],
                                                 identf), sem=PEs)
                evac(b, bCOPY(xnfm32_t[:, quad * 512:(quad + 1) * 512], b.h[:]),
                     pe_val=pe_val)
            if t == 0:
                use_bank(pbS)
            pl.wait("tensor", dxs, xload)
            pl.wait("tensor", VEs, VEs.n)
            for kt in range(FT):
                router_pe = pl.op("tensor", bMM(pbS.h[:, t * 16:(t + 1) * 16],
                                                xnfm32_t[:, kt * 128:(kt + 1) * 128],
                                                rouT_sb[:, kt * 16:(kt + 1) * 16],
                                                kt == 0, kt == FT - 1), sem=PEs)
        evac(pbS, bCOPY(logit[:, 0:128], pbS.h[:, 0:128]), pe_val=router_pe)

        for t in range(TT):
            lt = logit[:, t * 16:(t + 1) * 16]
            p1c = wcols[:, t:t + 1]
            p2c = wcols[:, 8 + t:9 + t]
            i1c = stat[:, 40:41]
            m1 = m1_all[:, t * 16:(t + 1) * 16]
            m2 = m2_all[:, t * 16:(t + 1) * 16]
            pl.sop("vector", bRED(p1c, lt, OP.max), VEs)
            pl.sop("vector", bTSS(stat2[:, 0:16], lt, p1c, OP.is_ge), VEs)
            pl.sop("vector", bTSS(stat2[:, 16:32], iota16[:, 0:16], 1e4, OP.subtract), VEs)
            pl.sop("vector", bTT(stat2[:, 32:48], stat2[:, 16:32], stat2[:, 0:16],
                                 OP.mult), VEs)
            pl.sop("vector", bTSS(stat2[:, 48:64], stat2[:, 32:48], 1e4, OP.add), VEs)
            pl.sop("vector", bRED(i1c, stat2[:, 48:64], OP.min), VEs)
            pl.sop("vector", bTSS(stat2[:, 0:16], iota16[:, 0:16], i1c, OP.is_ge), VEs)
            pl.sop("vector", bTSS(stat2[:, 16:32], iota16[:, 0:16], i1c, OP.is_le), VEs)
            pl.sop("vector", bTT(m1, stat2[:, 0:16], stat2[:, 16:32], OP.mult), VEs)
            pl.sop("vector", bTSS(stat2[:, 32:48], m1, 1e6, OP.mult), VEs)
            pl.sop("vector", bTT(stat2[:, 48:64], lt, stat2[:, 32:48], OP.subtract), VEs)
            pl.sop("vector", bRED(p2c, stat2[:, 48:64], OP.max), VEs)
            pl.sop("vector", bTSS(stat2[:, 0:16], stat2[:, 48:64], p2c, OP.is_ge), VEs)
            pl.sop("vector", bTSS(stat2[:, 16:32], iota16[:, 0:16], 1e4, OP.subtract), VEs)
            pl.sop("vector", bTT(stat2[:, 32:48], stat2[:, 16:32], stat2[:, 0:16],
                                 OP.mult), VEs)
            pl.sop("vector", bTSS(stat2[:, 48:64], stat2[:, 32:48], 1e4, OP.add), VEs)
            pl.sop("vector", bRED(i1c, stat2[:, 48:64], OP.min), VEs)
            pl.sop("vector", bTSS(stat2[:, 0:16], iota16[:, 0:16], i1c, OP.is_ge), VEs)
            pl.sop("vector", bTSS(stat2[:, 16:32], iota16[:, 0:16], i1c, OP.is_le), VEs)
            pl.sop("vector", bTT(m2, stat2[:, 0:16], stat2[:, 16:32], OP.mult), VEs)
        dv = pl.sop("vector", bTT(wcols[:, 16:24], wcols[:, 0:8], wcols[:, 8:16],
                                  OP.subtract), VEs)
        pl.wait("scalar", VEs, dv)
        w1v = pl.op("scalar", bACT(wcols[:, 0:8], wcols[:, 16:24], AF.Sigmoid), sem=SEs)
        pl.wait("vector", SEs, w1v)
        pl.sop("vector", bTS(wcols[:, 8:16], wcols[:, 0:8], -1.0, 1.0, OP.mult, OP.add), VEs)
        for t in range(TT):
            pl.sop("vector", bTSS(stat2[:, 0:16], m1_all[:, t * 16:(t + 1) * 16],
                                  wcols[:, t:t + 1], OP.mult), VEs)
            pl.sop("vector", bTSS(stat2[:, 16:32], m2_all[:, t * 16:(t + 1) * 16],
                                  wcols[:, 8 + t:9 + t], OP.mult), VEs)
            pl.sop("vector", bTT(mask_tok[:, t * 16:(t + 1) * 16], stat2[:, 0:16],
                                 stat2[:, 16:32], OP.add), VEs)
        mask_done = VEs.n
        xn_done = VEs.n

        C = 256
        IOA = bass.IndirectOffsetOnAxis
        ohv = pl.sop("vector", bTT(oh_bf[:], m1_all[:], m2_all[:], OP.add), VEs)
        bC = next_bank()
        use_bank(bC)
        pl.wait("tensor", VEs, ohv)
        pl.wait("tensor", dxs, xload)
        pe_val = None
        for tt in range(TT):
            pe_val = pl.op("tensor", bMM(bC.h[:, tt * 16:(tt + 1) * 16], cumlt_sb[:],
                                         oh_bf[:, tt * 16:(tt + 1) * 16], True, True),
                           sem=PEs)
        evac(bC, bCOPY(ccum_f[:], bC.h[:, 0:128]), pe_val=pe_val)
        bT = next_bank()
        use_bank(bT)
        pl.wait("tensor", VEs, ones_v)
        pe_val = None
        for tt in range(TT):
            pe_val = pl.op("tensor", bMM(bT.h[0:16, tt:tt + 1],
                                         oh_bf[:, tt * 16:(tt + 1) * 16],
                                         onescol_bf[:], True, True), sem=PEs)
        evac(bT, bCOPY(tots_f[:], bT.h[0:16, 0:8]), pe_val=pe_val)
        bT2 = next_bank()
        use_bank(bT2)
        pl.wait("tensor", VEs, VEs.n)
        pe_val = pl.op("tensor", bTR(bT2.h[0:8, 0:16], tots_f[:], identf[0:16, 0:16]),
                       sem=PEs)
        evac(bT2, bCOPY(totsT_bf[:], bT2.h[0:8, 0:16]), pe_val=pe_val)
        bT3 = next_bank()
        use_bank(bT3)
        pl.wait("tensor", VEs, VEs.n)
        pe_val = pl.op("tensor", bMM(bT3.h[0:16, 0:8], totsT_bf[:], u8_sb[:],
                                     True, True), sem=PEs)
        evac(bT3, bCOPY(texE_f[:], bT3.h[0:16, 0:8]), pe_val=pe_val)
        bT4 = next_bank()
        use_bank(bT4)
        pl.wait("tensor", VEs, VEs.n)
        pe_val = pl.op("tensor", bTR(bT4.h[0:8, 0:16], texE_f[:], identf[0:16, 0:16]),
                       sem=PEs)
        evac(bT4, bCOPY(texT_f[:], bT4.h[0:8, 0:16]), pe_val=pe_val)
        pl.wait("sync", VEs, VEs.n)
        texS = get_bsem("texrow")
        tex_v = pl.op("sync", bDMA(texrow[0:1, :], texT_f[:]), sem=texS, k=16)
        bT5 = next_bank()
        use_bank(bT5)
        pl.wait("tensor", texS, tex_v)
        pe_val = pl.op("tensor", bMM(bT5.h[:, 0:128], onesf[0:1, :], texrow[0:1, :],
                                     True, True), sem=PEs)
        pl.wait("vector", PEs, pe_val)
        rk_v = pl.sop("vector", bTT(rank_f[:], ccum_f[:], bT5.h[:, 0:128], OP.add), VEs)
        bT5.free_sem, bT5.free_val = VEs, rk_v
        pl.sop("vector", bTT(logit[:], rank_f[:], ec_sb[:], OP.add), VEs)
        pl.sop("vector", bTSS(ccum_f[:], rank_f[:], float(C) - 0.5, OP.is_ge), VEs)
        pl.sop("vector", bTSS(rank_f[:], ccum_f[:], 16384.0, OP.mult), VEs)
        pl.sop("vector", bTT(ccum_f[:], logit[:], rank_f[:], OP.add), VEs)
        for mm_all, dst_f, dst_i in ((m1_all, destA_f, destA_i),
                                     (m2_all, destB_f, destB_i)):
            pl.sop("vector", bTT(logit[:], mm_all[:], ccum_f[:], OP.mult), VEs)
            for tt in range(TT):
                pl.sop("vector", bRED(dst_f[:, tt:tt + 1],
                                      logit[:, tt * 16:(tt + 1) * 16], OP.add), VEs)
            pl.sop("vector", bCOPY(dst_i[:], dst_f[:]), VEs)
        idx_v = VEs.n

        scatS = get_bsem("scat")
        pl.ops["gpsimd"].append(mk_reg("bc4095", 4095))
        pl.ops["gpsimd"].append(mk_reg("bc2047", 2047))
        pl.wait("gpsimd", VEs, idx_v)
        pl.wait("gpsimd", initS, initS.n)
        for tt in range(TT):
            pl.sdma("gpsimd", lambda E, o=dXs.ap(), i=destA_i[:, tt:tt + 1],
                  s=xnbf_all[tt][:]: E.indirect_dma_start(
                o, IOA(i, 0), s, None, bounds_check=regbox["bc4095"], oob_is_err=False),
                sem=scatS, k=16)
            pl.sdma("gpsimd", lambda E, o=dXs.ap(), i=destB_i[:, tt:tt + 1],
                  s=xnbf_all[tt][:]: E.indirect_dma_start(
                o, IOA(i, 0), s, None, bounds_check=regbox["bc4095"], oob_is_err=False),
                sem=scatS, k=16)
            pl.sdma("gpsimd", lambda E, o=dTok.ap(), i=destA_i[:, tt:tt + 1],
                  s=tokab_sb[:, tt:tt + 1]: E.indirect_dma_start(
                o, IOA(i, 0), s, None, bounds_check=regbox["bc4095"], oob_is_err=False),
                sem=scatS, k=16)
            pl.sdma("gpsimd", lambda E, o=dTok.ap(), i=destB_i[:, tt:tt + 1],
                  s=tokab_sb[:, 8 + tt:9 + tt]: E.indirect_dma_start(
                o, IOA(i, 0), s, None, bounds_check=regbox["bc4095"], oob_is_err=False),
                sem=scatS, k=16)
            pl.sdma("gpsimd", lambda E, o=dWsl.ap(), i=destA_i[:, tt:tt + 1],
                  s=wcols[:, tt:tt + 1]: E.indirect_dma_start(
                o, IOA(i, 0), s, None, bounds_check=regbox["bc4095"], oob_is_err=False),
                sem=scatS, k=16)
            pl.sdma("gpsimd", lambda E, o=dWsl.ap(), i=destB_i[:, tt:tt + 1],
                  s=wcols[:, 8 + tt:9 + tt]: E.indirect_dma_start(
                o, IOA(i, 0), s, None, bounds_check=regbox["bc4095"], oob_is_err=False),
                sem=scatS, k=16)
        scat_all = scatS.n

        phase_barrier()
        pl.wait("gpsimd", scatS, scat_all)
        pl.wait("gpsimd", SEs, SEs.n)
        memkh_v = pl.sdma("gpsimd", bDMA(mem_kh[:], dMemKH), sem=dwC, k=16)

        def proj_tok(wbuf, wsem, wval, bias_row, out_emit):
            for t in range(TT):
                b = next_bank()
                use_bank(b)
                pl.wait("tensor", wsem, wval)
                pl.wait("tensor", VEs, xn_done)
                pl.wait("tensor", dwC, dwC.n)
                pe_val = None
                for kt in range(FT):
                    pe_val = pl.op("tensor", bMM(
                        b.h[:],
                        xn_fm[:, kt * T + t * 128:kt * T + (t + 1) * 128],
                        wbuf[:, kt * 512:(kt + 1) * 512],
                        kt == 0, False), sem=PEs)
                pe_val = pl.op("tensor", bMM(b.h[:], onesb[0:1, :], bias_row,
                                             False, True), sem=PEs)
                out_emit(t, b, pe_val)

        def k_emit(t, b, pe_val):
            evac(b, bCOPY(k4_bf[:, t * 512:(t + 1) * 512], b.h[:]), pe_val=pe_val)

        proj_tok(wk_buf, wkS, wkV, bk_row[0:1, 0:512], k_emit)
        wused("rw0", PEs.n)

        def v_emit(t, b, pe_val):
            evac(b, bCOPY(v4_bf[:, t * 512:(t + 1) * 512], b.h[:]), pe_val=pe_val)

        pl.wait("vector", scatS, scat_all)
        pl.wait("scalar", scatS, scat_all)
        proj_tok(wv_buf, wvS, wvV, bv_row[0:1, 0:512], v_emit)
        wused("rw1", PEs.n)

        prev_kv_pe = 0
        prev_kv_ve = 0
        prev_kv_se = 0
        stage_cnt = 0
        for kv in range(HKV):
            pl.wait("vector", PEs, prev_kv_pe)
            pl.wait("vector", SEs, prev_kv_se)
            pl.wait("scalar", VEs, prev_kv_ve)
            pl.wait("scalar", PEs, prev_kv_pe)
            rkey = "rw%d" % (kv % 2)
            wmb_buf = carve(RW, (kv % 2) * 16384, 8192, BF16)
            wmbS, wmbV = wfill(rkey, bufview(wmb_buf), colsrc(dWmbT, kv))

            def mb_emit(t, b, pe_val):
                evac(b, bACT(mb4_bf[:, t * 512:(t + 1) * 512], b.h[:], AF.Sigmoid),
                     eng="scalar", pe_val=pe_val)

            proj_tok(wmb_buf, wmbS, wmbV, bmb_row[0:1, kv * 512:(kv + 1) * 512], mb_emit)
            wused(rkey, PEs.n)

            mk_v = emit_dpfp(
                lambda t, kv=kv: k4_bf[:, t * 512 + kv * 128:t * 512 + (kv + 1) * 128],
                a1_all, mk_tok, dump1[:], True)
            for k in range(FK):
                srcs = [mk_tok[:, t * FDIM + k * 128:t * FDIM + (k + 1) * 128]
                        for t in range(TT)]
                transpose_batch(srcs, mk_fm[:, k * T:(k + 1) * T], extra_wait=(VEs, mk_v))
            mkfm_v = VEs.n

            use_bank(pbS)
            norm4 = norm_bf.rearrange("p (h k) -> p h k", k=FK)
            pe_val = None
            pl.wait("tensor", VEs, mkfm_v)
            pl.wait("tensor", dwC, dwC.n)
            for t in range(TT):
                for k in range(FK):
                    pe_val = pl.op("tensor", bMM(
                        pbS.h[:, t * 4:(t + 1) * 4],
                        mk_fm[:, k * T + t * 128:k * T + (t + 1) * 128],
                        norm4[:, kv * 4:(kv + 1) * 4, k],
                        k == 0, k == FK - 1), sem=PEs)
            evac(pbS, bTSS(den4[:, 0:32], pbS.h[:, 0:32], EPS_A, OP.add), pe_val=pe_val)
            pl.sop("vector", bRECIP(drec4[:, 0:32], den4[:, 0:32]), VEs)
            den4v = den4.rearrange("p (t r) -> p t r", r=4)
            coef4v = coef4.rearrange("p (t r) -> p t r", r=4)
            for r in range(4):
                pl.sop("vector", bTT(stat[:, 0:8], den4v[:, :, r], mksqr[:, 0:8],
                                     OP.mult), VEs)
                pl.sop("vector", bTS(stat[:, 8:16], stat[:, 0:8], -1.0, 1.0,
                                     OP.mult, OP.add), VEs)
                pl.sop("vector", bTS(coef4v[:, :, r], stat[:, 8:16], 0.0, 1.0,
                                     OP.max, OP.min), VEs)
            coef_v = VEs.n

            for t in range(TT):
                b = next_bank()
                use_bank(b)
                pl.wait("tensor", dwC, memkh_v)
                pe_val = None
                for k in range(FK):
                    pe_val = pl.op("tensor", bMM(
                        b.h[:],
                        mk_fm[:, k * T + t * 128:k * T + (t + 1) * 128],
                        mem_kh[:, (k * 16 + kv * 4) * 128:(k * 16 + kv * 4 + 4) * 128],
                        k == 0, k == FK - 1), sem=PEs)
                pl.wait("vector", PEs, pe_val)
                pl.wait("vector", VEs, coef_v)
                last_v = None
                for r in range(4):
                    pl.sop("vector", bTSS(pv_scr[:], b.h[:, r * 128:(r + 1) * 128],
                                          drec4[:, t * 4 + r:t * 4 + r + 1], OP.mult), VEs)
                    pl.sop("vector", bTT(mv_scr[:],
                                         v4_bf[:, t * 512 + kv * 128:t * 512 + (kv + 1) * 128],
                                         pv_scr[:], OP.subtract), VEs)
                    last_v = pl.sop("vector", bTT(
                        wmv4[:, t * 512 + r * 128:t * 512 + (r + 1) * 128], mv_scr[:],
                        mb4_bf[:, t * 512 + r * 128:t * 512 + (r + 1) * 128],
                        OP.mult), VEs)
                b.free_sem, b.free_val = VEs, last_v
            wmv_v = VEs.n

            for r in range(4):
                h = kv * 4 + r
                for rnd in range(2):
                    use_bank(pbD)
                    pl.wait("tensor", VEs, wmv_v)
                    pe_val = None
                    for kk in range(3):
                        k = rnd * 3 + kk
                        for t in range(TT):
                            pe_val = pl.op("tensor", bMM(
                                pbD.h[:, kk * 128:(kk + 1) * 128],
                                mk_tok[:, t * FDIM + k * 128:t * FDIM + (k + 1) * 128],
                                wmv4[:, t * 512 + r * 128:t * 512 + (r + 1) * 128],
                                t == 0, t == TT - 1), sem=PEs)
                        for t in range(TT):
                            pe_val = pl.op("tensor", bMM(
                                pbD.h[:, 384 + kk:385 + kk],
                                mk_tok[:, t * FDIM + k * 128:t * FDIM + (k + 1) * 128],
                                coef4[:, t * 4 + r:t * 4 + r + 1],
                                t == 0, t == TT - 1), sem=PEs)
                    pl.wait("vector", dxs, stage_cnt)
                    pl.wait("vector", PEs, pe_val)
                    pl.op("vector", bCOPY(stg_m[:, rnd * 384:(rnd + 1) * 384],
                                          pbD.h[:, 0:384]), sem=VEs)
                    v2 = pl.op("vector", bCOPY(stg_n[:, rnd * 3:(rnd + 1) * 3],
                                               pbD.h[:, 384:387]), sem=VEs)
                    pbD.free_sem, pbD.free_val = VEs, v2
                pl.wait("sync", VEs, VEs.n)
                pl.sdma("sync", bDMA(dCCmi.ap()[:, h * 768:(h + 1) * 768], stg_m[:]),
                      sem=dxs, k=16)
                pl.sdma("sync", bDMA(dCCni.ap()[:, h * 6:(h + 1) * 6], stg_n[:]),
                      sem=dxs, k=16)
                stage_cnt = dxs.n
            prev_kv_pe = PEs.n
            prev_kv_ve = VEs.n
            prev_kv_se = SEs.n

        pl.wait("gpsimd", dxs, stage_cnt)
        pl.op("gpsimd", lambda Eng: Eng.collective_compute(
            "AllReduce", OP.add, replica_groups=[list(range(NCORES))],
            ins=[dCCmi.ap().opt()], outs=[dCCmo.ap().opt()]), sem=csem)
        pl.op("gpsimd", lambda Eng: Eng.collective_compute(
            "AllReduce", OP.add, replica_groups=[list(range(NCORES))],
            ins=[dCCni.ap().opt()], outs=[dCCno.ap().opt()]), sem=csem)

        phase_barrier()
        ffn_acc = carve(RX, 0, 16384, F32)
        gq = [carve(RW, 0, 4096, BF16), carve(RW, 8192, 4096, BF16)]
        uq = [carve(RW, 16384, 4096, BF16), carve(RW, 24576, 4096, BF16)]
        h_all = carve(RH, 0, 8192, BF16)
        qctr = [0]
        prev_down_pe = [0]

        for ui in range(2):
            pl.wait("vector", PEs, prev_down_pe[0])
            for qi in range(4):
                par = qctr[0] % 2
                qctr[0] += 1
                gsrc = colsrc(dSgw, ui * 4 + qi, cols=256)
                usrc = colsrc(dSuw, ui * 4 + qi, cols=256)
                gS, gv = wfill("gq%d" % par, bufview(gq[par], cols=256), gsrc)
                uS, uv = wfill("uq%d" % par, bufview(uq[par], cols=256), usrc)
                for lhf in range(2):
                    hf = qi * 2 + lhf
                    for ch in range(2):
                        bG = next_bank()
                        bU = next_bank()
                        use_bank(bG)
                        use_bank(bU)
                        pl.wait("tensor", gS, gv)
                        pl.wait("tensor", uS, uv)
                        pl.wait("tensor", VEs, xn_done)
                        for kt in range(FT):
                            pl.op("tensor", bMM(
                                bG.h[:],
                                gq[par][:, kt * 256 + lhf * 128:kt * 256 + (lhf + 1) * 128],
                                xn_fm[:, kt * T + ch * 512:kt * T + (ch + 1) * 512],
                                kt == 0, kt == FT - 1), sem=PEs)
                            pe_val = pl.op("tensor", bMM(
                                bU.h[:],
                                uq[par][:, kt * 256 + lhf * 128:kt * 256 + (lhf + 1) * 128],
                                xn_fm[:, kt * T + ch * 512:kt * T + (ch + 1) * 512],
                                kt == 0, kt == FT - 1), sem=PEs)
                        pl.wait("scalar", PEs, pe_val)
                        pl.wait("scalar", VEs, VEs.n)
                        sv = pl.op("scalar", bACT(sil_scr[:], bG.h[:], AF.Silu), sem=SEs)
                        bG.free_sem, bG.free_val = SEs, sv
                        pl.wait("vector", SEs, sv)
                        hcol = hf * T + ch * 512
                        hv = pl.sop("vector", bTT(h_all[:, hcol:hcol + 512],
                                                  sil_scr[:], bU.h[:], OP.mult), VEs)
                        bU.free_sem, bU.free_val = VEs, hv
                wused("gq%d" % par, PEs.n)
                wused("uq%d" % par, PEs.n)
            h_done = VEs.n

            for oc in range(4):
                dpar = oc % 2
                dsrc = dSdw[ui * FE:(ui + 1) * FE,
                            oc * 512:(oc + 1) * 512].rearrange(
                    "(kt p) c -> p kt c", p=128)
                dS, dnv = wfill("dn%d" % dpar, bufview(dn_buf[dpar]), dsrc)
                for t in range(TT):
                    b = next_bank()
                    use_bank(b)
                    pl.wait("tensor", dS, dnv)
                    pl.wait("tensor", VEs, h_done)
                    pe_val = None
                    for kt in range(8):
                        pe_val = pl.op("tensor", bMM(
                            b.h[:],
                            h_all[:, kt * T + t * 128:kt * T + (t + 1) * 128],
                            dn_buf[dpar][:, kt * 512:(kt + 1) * 512],
                            kt == 0, kt == 7), sem=PEs)
                    aslice = ffn_acc[:, t * D + oc * 512:t * D + (oc + 1) * 512]
                    if ui == 0:
                        evac(b, bCOPY(aslice, b.h[:]), pe_val=pe_val)
                    else:
                        evac(b, bTT(aslice, aslice, b.h[:], OP.add), pe_val=pe_val)
                wused("dn%d" % dpar, PEs.n)
            prev_down_pe[0] = PEs.n
        pl.wait("sync", VEs, VEs.n)
        accoutS = get_bsem("accout")
        accout_v = pl.op("sync", bDMA(dAcc.ap(), ffn_acc[:]), sem=accoutS, k=16)

        xs_sb = carve(RH, 4096, 2048, BF16)
        xs_fm = carve(RH, 8192, 4096, BF16)
        h_e = carve(RH, 0, 2048, BF16)
        dn_sb = [[carve(RS, ep * 8192 + st * 4096, 2048, BF16) for st in range(2)]
                 for ep in range(2)]
        xsbS = get_bsem("xsb")
        tkS = [get_bsem("tk%d" % i) for i in range(32)]
        wsS2 = [get_bsem("ws%d" % i) for i in range(32)]
        scatBp = [get_bsem("scatb0"), get_bsem("scatb1")]
        xsb_pe = [prev_down_pe[0]]
        ring_last_scat = [0, 0, 0, 0]
        dnsb_last_scat = [0, 0]
        pend = [None]

        mark_i = [0]

        def flush_scatters():
            if pend[0] is None:
                return
            epar, info, dn_done = pend[0]
            pend[0] = None
            pl.wait("gpsimd", VEs, dn_done)
            for st, r, tv in info:
                sdud = get_bsem("scatdud%d" % st)
                pl.wait("gpsimd", tkS[r], tv)
                pl.op("gpsimd", lambda E, o=dBlob.ap(), i=tok_col[:, r:r + 1],
                      s=dn_sb[epar][st][:]: E.indirect_dma_start(
                    o, IOA(i, 0), s, None, bounds_check=regbox["bc2047"],
                    oob_is_err=False), sem=sdud, k=16)
            for st in range(2):
                sd = get_bsem("scatdud%d" % st)
                pl.wait("gpsimd", sd, sd.n)
            pl.wait("gpsimd", scatBp[epar], dnsb_last_scat[epar])
            v = pl.op("gpsimd", bDMA(dDummy.ap()[:, epar * 64:(epar + 1) * 64],
                                     carve(RS, epar * 8192, 8192, BF16)[:, 0:64]),
                      sem=scatBp[epar], k=16)
            dnsb_last_scat[epar] = v

        def emit_gufill(e, qi):
            par = qctr[0] % 2
            qctr[0] += 1
            gS, gv = wfill("gq%d" % par, bufview(gq[par], cols=256),
                           colsrc(dGateW[e], qi, cols=256))
            uS, uv = wfill("uq%d" % par, bufview(uq[par], cols=256),
                           colsrc(dUpW[e], qi, cols=256))
            return (par, gS, gv, uS, uv)

        for e in range(E):
            epar = e % 2
            fills = {0: emit_gufill(e, 0), 1: emit_gufill(e, 1)}

            info = []
            for st in range(2):
                pl.wait("sync", scatS, scat_all)
                pl.wait("sync", PEs, xsb_pe[0])
                xv = pl.op("sync", bDMA(
                    xs_sb[:],
                    dXs.ap().rearrange("(s p) c -> s p c", p=128)[e * 2 + st]),
                    sem=xsbS, k=16)
                for half in range(2):
                    srcs = [xs_sb[:, (half * 8 + j) * 128:(half * 8 + j + 1) * 128]
                            for j in range(8)]
                    dst = xs_fm.rearrange("p (k c) -> p k c", c=256)[
                        :, half * 8:(half + 1) * 8, st * 128:(st + 1) * 128]
                    transpose_batch(srcs, dst, extra_wait=(xsbS, xv))
                xsb_pe[0] = PEs.n
                r = e * 2 + st
                tv = pl.op("sync", bDMA(
                    tok_col[:, r:r + 1],
                    dTok.ap().rearrange("(s p) o -> s p o", p=128)[e * 2 + st]),
                    sem=tkS[r], k=16)
                wv2 = pl.op("sync", bDMA(
                    wsl_col[:, r:r + 1],
                    dWsl.ap().rearrange("(s p) o -> s p o", p=128)[e * 2 + st]),
                    sem=wsS2[r], k=16)
                info.append((st, r, tv, wv2))
            xsfm_v = VEs.n

            pl.wait("vector", PEs, prev_down_pe[0])
            for qi in range(4):
                par, gS, gv, uS, uv = fills[qi]
                del fills[qi]
                for lhf in range(2):
                    hf = qi * 2 + lhf
                    bG = next_bank()
                    bU = next_bank()
                    use_bank(bG)
                    use_bank(bU)
                    pl.wait("tensor", gS, gv)
                    pl.wait("tensor", uS, uv)
                    pl.wait("tensor", VEs, xsfm_v)
                    for kt in range(FT):
                        pl.op("tensor", bMM(
                            bG.h[:, 0:256],
                            gq[par][:, kt * 256 + lhf * 128:kt * 256 + (lhf + 1) * 128],
                            xs_fm[:, kt * 256:(kt + 1) * 256],
                            kt == 0, kt == FT - 1), sem=PEs)
                        pe_val = pl.op("tensor", bMM(
                            bU.h[:, 0:256],
                            uq[par][:, kt * 256 + lhf * 128:kt * 256 + (lhf + 1) * 128],
                            xs_fm[:, kt * 256:(kt + 1) * 256],
                            kt == 0, kt == FT - 1), sem=PEs)
                    pl.wait("scalar", PEs, pe_val)
                    pl.wait("scalar", VEs, VEs.n)
                    sv = pl.op("scalar", bACT(sil_scr[:, 0:256], bG.h[:, 0:256],
                                              AF.Silu), sem=SEs)
                    bG.free_sem, bG.free_val = SEs, sv
                    pl.wait("vector", SEs, sv)
                    hv = pl.sop("vector", bTT(h_e[:, hf * 256:(hf + 1) * 256],
                                              sil_scr[:, 0:256], bU.h[:, 0:256],
                                              OP.mult), VEs)
                    bU.free_sem, bU.free_val = VEs, hv
                wused("gq%d" % par, PEs.n)
                wused("uq%d" % par, PEs.n)
                if qi + 2 < 4:
                    fills[qi + 2] = emit_gufill(e, qi + 2)
                if qi == 0:
                    flush_scatters()
            h_done = VEs.n

            for oc in range(4):
                dpar = oc % 2
                dsrc = dDownW[e][:, oc * 512:(oc + 1) * 512].rearrange(
                    "(kt p) c -> p kt c", p=128)
                dS, dnv = wfill("dn%d" % dpar, bufview(dn_buf[dpar]), dsrc)
                for st, r, tv, wv2 in info:
                    b = next_bank()
                    use_bank(b)
                    pl.wait("tensor", dS, dnv)
                    pl.wait("tensor", VEs, h_done)
                    pe_val = None
                    for kt in range(8):
                        pe_val = pl.op("tensor", bMM(
                            b.h[:],
                            h_e[:, kt * 256 + st * 128:kt * 256 + (st + 1) * 128],
                            dn_buf[dpar][:, kt * 512:(kt + 1) * 512],
                            kt == 0, kt == 7), sem=PEs)
                    pl.wait("vector", wsS2[r], wv2)
                    pl.wait("vector", scatBp[epar], dnsb_last_scat[epar])
                    evac(b, lambda E, o=dn_sb[epar][st][:, oc * 512:(oc + 1) * 512],
                         i=b.h[:], w=wsl_col[:, r:r + 1]: E.tensor_single_scalar(
                        o, i, w, OP.mult), pe_val=pe_val)
                wused("dn%d" % dpar, PEs.n)
            pend[0] = (epar, [(st, r, tv) for st, r, tv, _ in info], VEs.n)
            prev_down_pe[0] = PEs.n
        flush_scatters()
        scatb_all = [(s, s.n) for s in scatBp]

        phase_barrier()
        pl.wait("vector", accoutS, accout_v)
        pl.wait("scalar", accoutS, accout_v)
        for s, vv in scatb_all:
            pl.wait("vector", s, vv)
            pl.wait("scalar", s, vv)
        pl.wait("gpsimd", csem, 2)
        nn_dma = pl.sdma("gpsimd", bDMA(ccn_sb[:], dCCno.ap()), sem=dwC, k=16)
        pl.wait("vector", dwC, nn_dma)
        pl.wait("vector", dxs, dxs.n)
        nn_v = pl.sop("vector", bTT(nn_bf[:], ccn_sb[:], norm_f[:], OP.add), VEs)

        newm2 = [newm_bf, carve(RD, 4608, 768, BF16)]
        sS = get_bsem("s_dma")
        last_newm = [0]
        last_s = [0]
        phi_cons = [0, 0]
        newm_cons = [0, 0]

        def q_emit(t, b, pe_val):
            evac(b, bCOPY(q4_bf[:, t * 512:(t + 1) * 512], b.h[:]), pe_val=pe_val)

        def g_emit(t, b, pe_val):
            evac(b, bACT(g4_bf[:, t * 512:(t + 1) * 512], b.h[:], AF.Sigmoid),
                 eng="scalar", pe_val=pe_val)

        def head_stageA(h, par):
            r = h % 4
            pl.wait("gpsimd", csem, 2)
            pl.wait("gpsimd", VEs, last_newm[0])
            pl.sdma("gpsimd", bDMA(ccm_sb[:], dCCmo.ap()[:, h * 768:(h + 1) * 768]),
                  sem=dwC, k=16)
            mhd_v = pl.sdma("gpsimd", bDMA(memh_sb[:], dMemHC[:, h * 768:(h + 1) * 768]),
                          sem=dwC, k=16)
            pl.wait("vector", dwC, mhd_v)
            pl.wait("vector", PEs, newm_cons[par])
            newm_v = pl.sop("vector", bTT(newm2[par][:], ccm_sb[:], memh_sb[:],
                                          OP.add), VEs)
            last_newm[0] = newm_v
            pl.wait("vector", PEs, phi_cons[par])
            mq_v = emit_dpfp(
                lambda t, r=r: q4_bf[:, t * 512 + r * 128:t * 512 + (r + 1) * 128],
                a4_all, None, dump4[:], False, phi_ap=phi4[par])
            return (mq_v, newm_v)

        def head_stageB(h, par, mq_v, newm_v):
            r = h % 4
            for k in range(FK):
                srcs = [phi4[par][:, t * FDIM + k * 128:t * FDIM + (k + 1) * 128]
                        for t in range(TT)]
                transpose_batch(srcs, mq_fm2[par][:, k * T:(k + 1) * T],
                                extra_wait=(VEs, mq_v))
            phi_cons[par] = PEs.n
            mqfm_v = VEs.n
            use_bank(pbS)
            pl.wait("tensor", VEs, max(mqfm_v, nn_v))
            pe_val = None
            for t in range(TT):
                for k in range(FK):
                    pe_val = pl.op("tensor", bMM(
                        pbS.h[:, t:t + 1],
                        mq_fm2[par][:, k * T + t * 128:k * T + (t + 1) * 128],
                        nn_bf[:, h * 6 + k:h * 6 + k + 1],
                        k == 0, k == FK - 1), sem=PEs)
            evac(pbS, bTSS(denac[:, 0:8], pbS.h[:, 0:8], EPS_A, OP.add),
                 pe_val=pe_val)
            pl.sop("vector", bRECIP(darec[:, 0:8], denac[:, 0:8]), VEs)
            darec_v = VEs.n
            srcs = [xn_fm[:, h * T + t * 128:h * T + (t + 1) * 128]
                    for t in range(TT)]
            xnt_v = transpose_batch(srcs, xnt_h[:])
            pl.wait("vector", sS, last_s[0])
            for t in range(TT):
                b = next_bank()
                use_bank(b)
                pl.wait("tensor", VEs, newm_v)
                pe_val = None
                for k in range(FK):
                    pe_val = pl.op("tensor", bMM(
                        b.h[:, 0:128],
                        mq_fm2[par][:, k * T + t * 128:k * T + (t + 1) * 128],
                        newm2[par][:, k * 128:(k + 1) * 128],
                        k == 0, k == FK - 1), sem=PEs)
                pl.wait("vector", PEs, pe_val)
                pl.wait("vector", VEs, max(darec_v, xnt_v))
                pl.sop("vector", bTSS(mh_scr[:], b.h[:, 0:128],
                                      darec[:, t:t + 1], OP.mult), VEs)
                pl.sop("vector", bTT(
                    sg_scr[:],
                    g4_bf[:, t * 512 + r * 128:t * 512 + (r + 1) * 128],
                    mh_scr[:], OP.mult), VEs)
                sv = pl.sop("vector", bTT(
                    s_h[:, t * 128:(t + 1) * 128],
                    xnt_h[:, t * 128:(t + 1) * 128], sg_scr[:], OP.add), VEs)
                b.free_sem, b.free_val = VEs, sv
            newm_cons[par] = PEs.n
            pl.wait("sync", VEs, VEs.n)
            pl.wait("sync", sS, sS.n)
            last_s[0] = pl.op("sync", bDMA(
                dSd.ap().rearrange("p (t g c) -> p t g c", g=16, c=128)[:, :, h, :],
                s_h[:].rearrange("p (t c) -> p t c", c=128)), sem=sS, k=16)

        pending = [None]
        for h in range(16):
            if h % 4 == 0:
                if pending[0] is not None:
                    head_stageB(*pending[0])
                    pending[0] = None
                kvg = h // 4
                pl.wait("vector", PEs, PEs.n)
                pl.wait("scalar", PEs, PEs.n)
                pl.wait("scalar", VEs, VEs.n)
                pl.wait("vector", SEs, SEs.n)
                wqS, wq_v = wfill("rw0", bufview(wq_buf), colsrc(dWqT, kvg))
                wgS, wg_v = wfill("rw1", bufview(wg_buf), colsrc(dWgT, kvg))
                proj_tok(wq_buf, wqS, wq_v, bq_row[0:1, kvg * 512:(kvg + 1) * 512],
                         q_emit)
                wused("rw0", PEs.n)
                proj_tok(wg_buf, wgS, wg_v, bg_row[0:1, kvg * 512:(kvg + 1) * 512],
                         g_emit)
                wused("rw1", PEs.n)
            mq_v, newm_v = head_stageA(h, h % 2)
            if pending[0] is not None:
                head_stageB(*pending[0])
            pending[0] = (h, h % 2, mq_v, newm_v)
        head_stageB(*pending[0])
        s_all = sS.n
        s_done = VEs.n

        phase_barrier()
        pl.wait("gpsimd", accoutS, accout_v)
        pl.wait("gpsimd", PEs, PEs.n)
        pl.wait("sync", PEs, PEs.n)
        for s, vv in scatb_all:
            pl.wait("sync", s, vv)
        pl.wait("sync", sS, s_all)
        s5 = [carve(RX, 49152, 2048, BF16), carve(RX, 53248, 2048, BF16)]
        s5S = [get_bsem("s5_%d" % i) for i in range(2)]
        accTS = [get_bsem("a5_%d" % i) for i in range(2)]
        x5S = [get_bsem("x5_%d" % i) for i in range(2)]
        outS = [get_bsem("o5_%d" % i) for i in range(2)]
        blAS = [get_bsem("bA_%d" % i) for i in range(2)]
        blBS = [get_bsem("bB_%d" % i) for i in range(2)]
        blA = [carve(RW, 0, 2048, BF16), carve(RW, 4096, 2048, BF16)]
        blB = [carve(RW, 8192, 2048, BF16), carve(RW, 12288, 2048, BF16)]
        fsum = carve(RS, 16384, 2048, F32)
        blobv = dBlob.ap().rearrange("(b p) c -> b p c", p=128)
        prev_ve_of = {}
        for t in range(TT):
            par = t % 2
            if t >= 2:
                pl.wait("gpsimd", VEs, prev_ve_of[t - 2])
                pl.wait("sync", VEs, prev_ve_of[t - 2])
            ac_v = pl.op("gpsimd", bDMA(acc_t[par][:], dAcc.ap()[:, t * D:(t + 1) * D]),
                         sem=accTS[par], k=16)
            bA_v = pl.op("sync", bDMA(blA[par][:], blobv[t]), sem=blAS[par], k=16)
            sl_v = pl.op("sync", bDMA(s5[par][:], dSd.ap()[:, t * D:(t + 1) * D]),
                         sem=s5S[par], k=16)
            bB_v = pl.op("sync", bDMA(blB[par][:], blobv[8 + t]), sem=blBS[par], k=16)
            x_v = pl.op("sync", bDMA(x5_t[par][:],
                                     dX.rearrange("(t p) c -> t p c", p=128)[t]),
                        sem=x5S[par], k=16)
            pl.wait("vector", accTS[par], ac_v)
            pl.wait("vector", blAS[par], bA_v)
            pl.wait("vector", blBS[par], bB_v)
            if t >= 2:
                pl.wait("vector", outS[par], 16 * (t // 2))
            pl.sop("vector", bTT(fsum[:], acc_t[par][:], blA[par][:], OP.add), VEs)
            f2 = pl.sop("vector", bTT(t2_scr[:], fsum[:], blB[par][:], OP.add), VEs)
            pl.wait("scalar", VEs, f2)
            si_v = pl.op("scalar", bACT(silu5[par][:], t2_scr[:], AF.Silu), sem=SEs)
            pl.wait("vector", SEs, si_v)
            pl.wait("vector", VEs, s_done)
            pl.wait("vector", x5S[par], x_v)
            pl.wait("vector", s5S[par], sl_v)
            pl.sop("vector", bTT(t2_scr[:], s5[par][:], silu5[par][:],
                                 OP.mult), VEs)
            ov = pl.sop("vector", bTT(out_st[par][:], x5_t[par][:], t2_scr[:], OP.add),
                        VEs)
            prev_ve_of[t] = ov
            pl.wait("sync", VEs, ov)
            pl.op("sync", bDMA(dOut.rearrange("(t p) c -> t p c", p=128)[t],
                               out_st[par][:]), sem=outS[par], k=16)
        pl.wait("sync", outS[0], 16 * 4)
        pl.wait("sync", outS[1], 16 * 4)

        if DEBUG:
            dbgS = get_bsem("dbg")
            o_dA = nc.dram_tensor("o_dA", [128, 8], I32, kind="ExternalOutput").ap()
            o_dB = nc.dram_tensor("o_dB", [128, 8], I32, kind="ExternalOutput").ap()
            o_w = nc.dram_tensor("o_w", [128, 32], F32, kind="ExternalOutput").ap()
            o_m = nc.dram_tensor("o_m", [128, 128], F32, kind="ExternalOutput").ap()
            o_tok = nc.dram_tensor("o_tok", [4096, 1], I32, kind="ExternalOutput").ap()
            o_wsl = nc.dram_tensor("o_wsl", [4096, 1], F32, kind="ExternalOutput").ap()
            o_blob = nc.dram_tensor("o_blob", [2048, 2048], BF16,
                                    kind="ExternalOutput").ap()
            for s, vv in scatb_all:
                pl.wait("sync", s, vv)
            pl.op("sync", bDMA(o_dA, destA_i[:]), sem=dbgS, k=16)
            pl.op("sync", bDMA(o_dB, destB_i[:]), sem=dbgS, k=16)
            pl.op("sync", bDMA(o_w, wcols[:]), sem=dbgS, k=16)
            pl.op("sync", bDMA(o_m, mask_tok[:]), sem=dbgS, k=16)
            pl.op("sync", bDMA(o_tok, dTok.ap()), sem=dbgS, k=16)
            pl.op("sync", bDMA(o_wsl, dWsl.ap()), sem=dbgS, k=16)
            pl.op("sync", bDMA(o_blob, dBlob.ap()), sem=dbgS, k=16)
            o_xs = nc.dram_tensor("o_xs", [4096, 2048], BF16, kind="ExternalOutput").ap()
            pl.op("sync", bDMA(o_xs, dXs.ap()), sem=dbgS, k=16)
            pl.wait("sync", dbgS, dbgS.n)

        with nc.Block() as block:

            @block.tensor
            def _(Eng):
                for f in pl.ops["tensor"]:
                    f(Eng)

            @block.vector
            def _(Eng):
                for f in pl.ops["vector"]:
                    f(Eng)

            @block.scalar
            def _(Eng):
                for f in pl.ops["scalar"]:
                    f(Eng)

            @block.gpsimd
            def _(Eng):
                for f in pl.ops["gpsimd"]:
                    f(Eng)

            @block.sync
            def _(Eng):
                for f in pl.ops["sync"]:
                    f(Eng)

    return nc


_NC_CACHE = {}


def kernel(L0_attn_hidden_states, L0_attn_Wq, L0_attn_bq, L0_attn_Wk, L0_attn_bk,
           L0_attn_Wv, L0_attn_bv, L0_attn_Wg, L0_attn_bg, L0_attn_Wmb, L0_attn_bmb,
           L0_attn_memory, L0_attn_norm, L0_ffn_router_w, L0_ffn_gate_w, L0_ffn_up_w,
           L0_ffn_down_w, L0_ffn_shared_gate_w, L0_ffn_shared_up_w,
           L0_ffn_shared_down_w):
    f32 = np.float32
    x = np.ascontiguousarray(np.asarray(L0_attn_hidden_states, f32).reshape(8192, D))
    wqT = np.ascontiguousarray(np.asarray(L0_attn_Wq, f32).T)
    wkT = np.ascontiguousarray(np.asarray(L0_attn_Wk, f32).T)
    wvT = np.ascontiguousarray(np.asarray(L0_attn_Wv, f32).T)
    wgT = np.ascontiguousarray(np.asarray(L0_attn_Wg, f32).T)
    wmbT = np.ascontiguousarray(np.asarray(L0_attn_Wmb, f32).T)
    mem = np.asarray(L0_attn_memory, f32)
    mem_kh = np.ascontiguousarray(
        mem.reshape(16, 6, 128, 128).transpose(2, 1, 0, 3).reshape(128, 12288))
    mem_hc = np.ascontiguousarray(
        mem.reshape(16, 6, 128, 128).transpose(2, 0, 1, 3).reshape(128, 12288))
    norm_pc = np.ascontiguousarray(
        np.asarray(L0_attn_norm, f32).reshape(16, 6, 128).transpose(2, 0, 1).reshape(128, 96))
    routerT = np.ascontiguousarray(np.asarray(L0_ffn_router_w, f32).T)
    ident = np.eye(128, dtype=BF)
    identf = np.eye(128, dtype=f32)
    iota16 = np.ascontiguousarray(np.tile(np.arange(16, dtype=f32)[None, :], (128, 1)))
    ii = np.arange(128)
    cumlt = (ii[:, None] < ii[None, :]).astype(BF)
    j8 = np.arange(8)
    u8t = (j8[:, None] < j8[None, :]).astype(BF)
    ec256 = np.ascontiguousarray(
        np.tile((np.arange(16, dtype=f32) * 256.0)[None, :], (128, 8)).reshape(128, 128))
    tok = (np.arange(8)[None, :] * 128 + ii[:, None]).astype(np.int32)
    tokab = np.ascontiguousarray(np.concatenate([tok, tok + 1024], axis=1))

    common = {
        "wqT": wqT, "wkT": wkT, "wvT": wvT, "wgT": wgT, "wmbT": wmbT,
        "bk": np.asarray(L0_attn_bk, f32), "bv": np.asarray(L0_attn_bv, f32),
        "bq": np.asarray(L0_attn_bq, f32), "bg": np.asarray(L0_attn_bg, f32),
        "bmb": np.asarray(L0_attn_bmb, f32),
        "mem_kh": mem_kh, "mem_hc": mem_hc, "norm_pc": norm_pc,
        "routerT": routerT,
        "gate_w": np.asarray(L0_ffn_gate_w, f32),
        "up_w": np.asarray(L0_ffn_up_w, f32),
        "down_w": np.asarray(L0_ffn_down_w, f32),
        "sgw": np.asarray(L0_ffn_shared_gate_w, f32),
        "suw": np.asarray(L0_ffn_shared_up_w, f32),
        "sdw": np.asarray(L0_ffn_shared_down_w, f32),
        "ident": ident, "identf": identf, "iota16": iota16,
        "cumlt": cumlt, "u8t": u8t, "ec256": ec256, "tokab": tokab,
    }
    in_maps = []
    for c in range(NCORES):
        m = dict(common)
        m["x"] = np.ascontiguousarray(x[c * T:(c + 1) * T])
        in_maps.append(m)

    key = "nc_dbg" if DEBUG else "nc"
    if key not in _NC_CACHE:
        _NC_CACHE[key] = build_program()
    nc = _NC_CACHE[key]
    res = run_bass_kernel_spmd(nc, in_maps, core_ids=list(range(NCORES)))
    _LAST_RES[0] = res
    out = np.concatenate([np.asarray(res.results[c]["out"]) for c in range(NCORES)],
                         axis=0)
    return out.reshape(1, 8192, D).astype(np.float32)


# version 22
# speedup vs baseline: 1.0228x; 1.0228x over previous
import numpy as np
import ml_dtypes
import concourse.bass as bass
import concourse.mybir as mybir
from concourse.bass_utils import run_bass_kernel_spmd
from contextlib import ExitStack

F32 = mybir.dt.float32
BF16 = mybir.dt.bfloat16
U8 = mybir.dt.uint8
AF = mybir.ActivationFunctionType
AX = mybir.AxisListType
OP = mybir.AluOpType

NCORES = 8
T = 1024
TT = 8
D = 2048
FT = 16
HKV = 4
NU = 3
FDIM = 768
FK = 6
E = 16
FE = 1024
FSH = 2048
EPS_A = 1e-8
EPS_LN = 1e-5
L2EPS = 1e-12
BF = ml_dtypes.bfloat16


class Sem:
    def __init__(self, pl, name):
        self.name = name
        self.h = pl.stack.enter_context(pl.nc.semaphore(name))
        self.n = 0


class Plan:
    ENGS = ("tensor", "vector", "scalar", "gpsimd", "sync")

    def __init__(self, nc, stack):
        self.nc = nc
        self.stack = stack
        self.ops = {k: [] for k in self.ENGS}
        self.waited = {k: {} for k in self.ENGS}

    def sem(self, name):
        return Sem(self, name)

    def wait(self, eng, sem, val):
        if sem is None or val is None or val <= 0:
            return
        if self.waited[eng].get(sem.name, 0) >= val:
            return
        self.waited[eng][sem.name] = val
        self.ops[eng].append(lambda Eng, h=sem.h, v=val: Eng.wait_ge(h, v))

    def op(self, eng, build, sem=None, k=1):
        if sem is None:
            self.ops[eng].append(build)
            return None
        sem.n += k
        val = sem.n
        self.ops[eng].append(lambda Eng, b=build, h=sem.h, kk=k: b(Eng).then_inc(h, kk))
        return val

    def sop(self, eng, build, sem):
        self.wait(eng, sem, sem.n)
        return self.op(eng, build, sem=sem)

    def sdma(self, eng, build, sem, k=16):
        self.wait(eng, sem, sem.n)
        return self.op(eng, build, sem=sem, k=k)


def bMM(o, l, r, s, p):
    return lambda E: E.matmul(o, l, r, start=s, stop=p)


def bTR(o, i, ident):
    return lambda E: E.transpose(o, i, ident)


def bDMA(o, i, **kw):
    return lambda E: E.dma_start(out=o, in_=i, **kw)


def bTT(o, a, b, opk):
    return lambda E: E.tensor_tensor(o, a, b, opk)


def bTS(o, a, s1, s2, o0, o1):
    return lambda E: E.tensor_scalar(o, a, s1, s2, o0, o1)


def bTSS(o, a, s, opk):
    return lambda E: E.tensor_single_scalar(o, a, s, opk)


def bCOPY(o, a):
    return lambda E: E.tensor_copy(o, a)


def bRED(o, a, opk):
    return lambda E: E.tensor_reduce(o, a, axis=AX.X, op=opk)


def bACT(o, a, f, bias=0.0, scale=1.0, accum=None):
    return lambda E: E.activation(o, a, f, bias=bias, scale=scale, accum_out=accum)


def bRECIP(o, a):
    return lambda E: E.reciprocal(o, a)


def bMEMSET(a, c):
    return lambda E: E.memset(a, c)


def bSTT(o, a, s, b, o0, o1):
    return lambda E: E.scalar_tensor_tensor(o, a, s, b, o0, o1)


class Bank:
    def __init__(self, handle):
        self.h = handle
        self.free_sem = None
        self.free_val = 0


DEBUG = False
_LAST_RES = [None]


def build_program():
    nc = bass.Bass("TRN2", target_bir_lowering=False, debug=False)

    dX = nc.dram_tensor("x", [T, D], F32, kind="ExternalInput").ap()
    dWkT = nc.dram_tensor("wkT", [D, 512], F32, kind="ExternalInput").ap()
    dWvT = nc.dram_tensor("wvT", [D, 512], F32, kind="ExternalInput").ap()
    dWqT = nc.dram_tensor("wqT", [D, D], F32, kind="ExternalInput").ap()
    dWgT = nc.dram_tensor("wgT", [D, D], F32, kind="ExternalInput").ap()
    dWmbT = nc.dram_tensor("wmbT", [D, D], F32, kind="ExternalInput").ap()
    dBk = nc.dram_tensor("bk", [512], F32, kind="ExternalInput").ap()
    dBv = nc.dram_tensor("bv", [512], F32, kind="ExternalInput").ap()
    dBq = nc.dram_tensor("bq", [D], F32, kind="ExternalInput").ap()
    dBg = nc.dram_tensor("bg", [D], F32, kind="ExternalInput").ap()
    dBmb = nc.dram_tensor("bmb", [D], F32, kind="ExternalInput").ap()
    dMemKH = nc.dram_tensor("mem_kh", [128, 12288], F32, kind="ExternalInput").ap()
    dMemHC = nc.dram_tensor("mem_hc", [128, 12288], F32, kind="ExternalInput").ap()
    dNormPC = nc.dram_tensor("norm_pc", [128, 96], F32, kind="ExternalInput").ap()
    dRouT = nc.dram_tensor("routerT", [D, 16], F32, kind="ExternalInput").ap()
    dGateW = nc.dram_tensor("gate_w", [E, D, FE], F32, kind="ExternalInput").ap()
    dUpW = nc.dram_tensor("up_w", [E, D, FE], F32, kind="ExternalInput").ap()
    dDownW = nc.dram_tensor("down_w", [E, FE, D], F32, kind="ExternalInput").ap()
    dSgw = nc.dram_tensor("sgw", [D, FSH], F32, kind="ExternalInput").ap()
    dSuw = nc.dram_tensor("suw", [D, FSH], F32, kind="ExternalInput").ap()
    dSdw = nc.dram_tensor("sdw", [FSH, D], F32, kind="ExternalInput").ap()
    dIdent = nc.dram_tensor("ident", [128, 128], BF16, kind="ExternalInput").ap()
    dIdentF = nc.dram_tensor("identf", [128, 128], F32, kind="ExternalInput").ap()
    dIota = nc.dram_tensor("iota16", [128, 16], F32, kind="ExternalInput").ap()
    dOut = nc.dram_tensor("out", [T, D], F32, kind="ExternalOutput").ap()

    dCumLT = nc.dram_tensor("cumlt", [128, 128], BF16, kind="ExternalInput").ap()
    dU8 = nc.dram_tensor("u8t", [8, 8], BF16, kind="ExternalInput").ap()
    dEc = nc.dram_tensor("ec256", [128, 128], F32, kind="ExternalInput").ap()
    dTokAB = nc.dram_tensor("tokab", [128, 16], mybir.dt.int32, kind="ExternalInput").ap()

    dAcc = nc.dram_tensor("acc_d", [128, 16384], F32, kind="Internal")
    dDummy = nc.dram_tensor("dummy_d", [128, 128], BF16, kind="Internal")
    dSd = nc.dram_tensor("s_d", [128, 16384], BF16, kind="Internal")
    dXs = nc.dram_tensor("xs_d", [4096, 2048], BF16, kind="Internal")
    dTok = nc.dram_tensor("tok_d", [4096, 1], mybir.dt.int32, kind="Internal")
    dWsl = nc.dram_tensor("wsl_d", [4096, 1], F32, kind="Internal")
    dBlob = nc.dram_tensor("blob_d", [2048, 2048], BF16, kind="Internal")
    dCCmi = nc.dram_tensor("cc_mem_in", [128, 12288], F32, kind="Internal")
    dCCmo = nc.dram_tensor("cc_mem_out", [128, 12288], F32, kind="Internal")
    dCCni = nc.dram_tensor("cc_norm_in", [128, 96], F32, kind="Internal")
    dCCno = nc.dram_tensor("cc_norm_out", [128, 96], F32, kind="Internal")

    stack = ExitStack()
    with stack:
        al = nc.alloc_sbuf_tensor

        ident = al("ident_sb", [128, 128], BF16).ap()
        identf = al("identf_sb", [128, 128], F32).ap()
        onesb = al("ones_bf", [1, 128], BF16).ap()
        onesf = al("ones_f32", [1, 128], F32).ap()
        iota16 = al("iota_sb", [128, 16], F32).ap()
        bk_row = al("bk_row", [1, 512], BF16).ap()
        bv_row = al("bv_row", [1, 512], BF16).ap()
        bq_row = al("bq_row", [1, D], BF16).ap()
        bg_row = al("bg_row", [1, D], BF16).ap()
        bmb_row = al("bmb_row", [1, D], BF16).ap()
        norm_bf = al("norm_bf", [128, 96], BF16).ap()
        norm_f = al("norm_f", [128, 96], F32).ap()
        rouT_sb = al("rouT_sb", [128, 256], F32).ap()
        sx = al("sx", [128, 8], F32).ap()
        sxx = al("sxx", [128, 8], F32).ap()
        stat = al("stat", [128, 48], F32).ap()
        stat2 = al("stat2", [128, 96], F32).ap()
        ln_rinv = al("ln_rinv", [128, 8], F32).ap()
        ln_nmr = al("ln_nmr", [128, 8], F32).ap()
        m1_all = al("m1_all", [128, 128], F32).ap()
        m2_all = al("m2_all", [128, 128], F32).ap()
        mask_tok = al("mask_tok", [128, 128], F32).ap()
        wcols = al("wcols", [128, 32], F32).ap()
        ssc = al("ssc", [128, 8], F32).ap()
        rinvc = al("rinvc", [128, 8], F32).ap()
        mksqr = al("mksqr", [128, 8], F32).ap()
        den4 = al("den4", [128, 32], F32).ap()
        drec4 = al("drec4", [128, 32], F32).ap()
        coef4 = al("coef4", [128, 32], BF16).ap()
        denac = al("denac", [128, 8], F32).ap()
        darec = al("darec", [128, 8], F32).ap()
        nn_bf = al("nn_bf", [128, 96], BF16).ap()
        ccn_sb = al("ccn_sb", [128, 96], F32).ap()
        epsln_c = al("epsln_c", [128, 1], F32).ap()
        I32 = mybir.dt.int32
        u8_sb = al("u8_sb", [8, 8], BF16).ap()
        tokab_sb = al("tokab_sb", [128, 16], I32).ap()
        onescol_bf = al("onescol_bf", [128, 1], BF16).ap()
        tots_f = al("tots_f", [16, 8], F32).ap()
        totsT_bf = al("totsT_bf", [8, 16], BF16).ap()
        texE_f = al("texE_f", [16, 8], F32).ap()
        texT_f = al("texT_f", [8, 16], F32).ap()
        texrow = al("texrow", [1, 128], F32).ap()
        destA_f = al("destA_f", [128, 8], F32).ap()
        destB_f = al("destB_f", [128, 8], F32).ap()
        destA_i = al("destA_i", [128, 8], I32).ap()
        destB_i = al("destB_i", [128, 8], I32).ap()
        bigi32 = al("bigi32", [128, 32], I32).ap()
        tok_col = al("tok_col", [128, 32], I32).ap()
        wsl_col = al("wsl_col", [128, 32], F32).ap()
        pv_scr = al("pv_scr", [128, 128], F32).ap()
        mv_scr = al("mv_scr", [128, 128], F32).ap()
        mh_scr = al("mh_scr", [128, 128], F32).ap()
        sg_scr = al("sg_scr", [128, 128], F32).ap()

        xn_fm = al("xn_fm", [128, FT * T], BF16).ap()
        RX = al("RX", [128, 65536], U8)
        RS = al("RS", [128, 24576], U8)
        RW = al("RW", [128, 32768], U8)
        RD = al("RD", [128, 16384], U8)
        RH = al("RH", [128, 16384], U8)

        def carve(region, off, n, dt):
            size = 4 if dt == F32 else 2
            return region.ap()[:, off:off + n * size].bitcast(dt)

        x_all = carve(RX, 0, 16384, F32)
        xn32_scr = carve(RS, 0, 2048, F32)
        xnbf_all = [carve(RX, t * 8192 + 4096, 2048, BF16) for t in range(TT)]
        xnfm32_t = carve(RS, 12288, 2048, F32)
        dump0 = carve(RS, 20480, 2048, BF16)
        mem_kh = carve(RX, 0, 12288, BF16)
        mk_tok = carve(RX, 24576, 6144, BF16)
        mk_fm = carve(RX, 36864, 6144, BF16)
        v4_bf = carve(RX, 49152, 4096, BF16)
        mb4_bf = carve(RX, 57344, 4096, BF16)
        wmv4 = carve(RH, 0, 4096, BF16)
        k4_bf = carve(RH, 8192, 4096, BF16)
        phi_bf = carve(RS, 0, 6144, BF16)
        a_scr = carve(RS, 12288, 256, F32)
        dump1 = carve(RS, 13568, 768, BF16)
        a1_all = carve(RD, 4096, 2048, BF16)
        stg_m = carve(RD, 0, 768, F32)
        stg_n = carve(RD, 3072, 6, F32)
        ccum_f = carve(RD, 4096, 128, F32)
        rank_f = carve(RD, 4608, 128, F32)
        oh_bf = carve(RD, 5120, 128, BF16)
        ec_sb = carve(RD, 5632, 128, F32)
        cumlt_sb = carve(RD, 6144, 128, BF16)
        logit = carve(RD, 6656, 128, F32)
        zrob = carve(RD, 8192, 2048, BF16)
        zro8 = carve(RH, 0, 8192, BF16)
        gate_buf = carve(RW, 0, 8192, BF16)
        up_buf = carve(RW, 16384, 8192, BF16)
        dn_buf = [carve(RD, 0, 4096, BF16), carve(RD, 8192, 4096, BF16)]
        acst = [carve(RS, i * 2048, 512, F32) for i in range(4)]
        mask_em = carve(RS, 8192, 1024, F32)[0:16, :]
        mrow1 = carve(RS, 12288, 1024, F32)[0:1, :]
        mb_sb = carve(RS, 16384, 1024, F32)
        sil_scr = carve(RS, 20480, 512, BF16)
        h1_scr = carve(RS, 21504, 512, BF16)
        phi4 = [carve(RS, 0, 6144, BF16), carve(RX, 0, 6144, BF16)]
        mq_fm2 = [carve(RX, 12288, 6144, BF16), carve(RX, 32768, 6144, BF16)]
        q4_bf = carve(RX, 24576, 4096, BF16)
        s_h = carve(RX, 45056, 1024, BF16)
        wq_buf = carve(RW, 0, 8192, BF16)
        wg_buf = carve(RW, 16384, 8192, BF16)
        ccm_sb = carve(RD, 0, 768, F32)
        newm_bf = carve(RD, 6144, 768, BF16)
        memh_sb = carve(RD, 12288, 768, F32)
        dump4 = carve(RD, 3072, 768, BF16)
        a4_all = carve(RD, 7680, 2048, BF16)
        g4_bf = carve(RS, 13312, 4096, BF16)
        xnt_h = carve(RS, 21504, 1024, BF16)
        acc_t = [carve(RX, 0, 2048, F32), carve(RX, 8192, 2048, F32)]
        x5_t = [carve(RX, 16384, 2048, F32), carve(RX, 24576, 2048, F32)]
        silu5 = [carve(RS, 8192, 2048, BF16), carve(RS, 12288, 2048, BF16)]
        t2_scr = carve(RS, 0, 2048, F32)
        out_st = [carve(RD, 0, 2048, F32), carve(RD, 8192, 2048, F32)]

        pb = [Bank(nc.alloc_psum_tensor(f"pb{i}", [128, 512], F32).ap()) for i in range(5)]
        pbS = Bank(nc.alloc_psum_tensor("pbS", [128, 512], F32).ap())
        pbD = Bank(nc.alloc_psum_tensor("pbD", [128, 512], F32).ap())
        ptr = Bank(nc.alloc_psum_tensor("ptr", [128, 1024], BF16).ap())

        pl = Plan(nc, stack)
        PEs = pl.sem("PEs")
        VEs = pl.sem("VEs")
        SEs = pl.sem("SEs")
        dwC = pl.sem("dwC")
        dxs = pl.sem("dxs")
        csem = pl.sem("csem")
        bsem = {}

        def get_bsem(key):
            if key not in bsem:
                bsem[key] = pl.sem("b_" + key)
            return bsem[key]

        rr = [0]
        regbox = {}

        def mk_reg(key, val):
            def f(Eng, k=key, v=val):
                regbox[k] = Eng.to_reg(v)
            return f

        def next_bank():
            b = pb[rr[0] % len(pb)]
            rr[0] += 1
            return b

        def use_bank(b):
            if b.free_sem is not None:
                pl.wait("tensor", b.free_sem, b.free_val)

        def evac(b, build, eng="vector", pe_val=None):
            s = VEs if eng == "vector" else SEs
            pl.wait(eng, PEs, pe_val)
            v = pl.op(eng, build, sem=s)
            b.free_sem, b.free_val = s, v
            return v

        consts_x_box = [0]

        def transpose_batch(srcs, dst, idn=None, extra_wait=None):
            use_bank(ptr)
            pl.wait("tensor", dxs, consts_x_box[0])
            if extra_wait is not None:
                pl.wait("tensor", extra_wait[0], extra_wait[1])
            pe_val = None
            for i, s in enumerate(srcs):
                o = ptr.h[:, i * 128:(i + 1) * 128]
                pe_val = pl.op("tensor", bTR(o, s, idn if idn is not None else ident),
                               sem=PEs)
            pl.wait("vector", PEs, pe_val)
            v = pl.op("vector", bCOPY(dst, ptr.h[:, 0:len(srcs) * 128]), sem=VEs)
            ptr.free_sem, ptr.free_val = VEs, v
            return v

        def emit_dpfp(src_fn, a_all, mk_dst, dump_ap, with_mksq, phi_ap=None):
            phi = phi_ap if phi_ap is not None else phi_bf
            if mk_dst is None:
                mk_dst = phi
            pl.wait("vector", VEs, VEs.n)
            pl.wait("vector", SEs, SEs.n)
            for t in range(TT):
                ab = t * 256
                pl.op("vector", bTSS(a_all[:, ab:ab + 128], src_fn(t), 0.0, OP.max),
                      sem=VEs)
                pl.op("vector", bTS(a_all[:, ab + 128:ab + 256], src_fn(t), -1.0, 0.0,
                                    OP.mult, OP.max), sem=VEs)
            pl.wait("vector", VEs, VEs.n)
            for t in range(TT):
                base = t * FDIM
                ab = t * 256
                for j in range(1, NU + 1):
                    o0 = base + (j - 1) * 256
                    pl.op("vector", bTT(phi[:, o0:o0 + j], a_all[:, ab:ab + j],
                                        a_all[:, ab + 256 - j:ab + 256], OP.mult),
                          sem=VEs)
                    pl.op("vector", bTT(phi[:, o0 + j:o0 + 256],
                                        a_all[:, ab + j:ab + 256],
                                        a_all[:, ab:ab + 256 - j], OP.mult), sem=VEs)
            phi_v = VEs.n
            pl.wait("scalar", VEs, phi_v)
            for t in range(TT):
                pl.sop("scalar", bACT(dump_ap, phi[:, t * FDIM:(t + 1) * FDIM],
                                      AF.Square, accum=ssc[:, t:t + 1]), SEs)
            pl.wait("scalar", SEs, SEs.n)
            sq_v = pl.op("scalar", bACT(stat[:, 0:8], ssc[:, 0:8], AF.Sqrt), sem=SEs)
            pl.wait("vector", SEs, sq_v)
            pl.sop("vector", bTSS(stat[:, 8:16], stat[:, 0:8], L2EPS, OP.max), VEs)
            pl.sop("vector", bRECIP(rinvc[:, 0:8], stat[:, 8:16]), VEs)
            if with_mksq:
                pl.sop("vector", bTT(stat[:, 16:24], rinvc[:, 0:8], rinvc[:, 0:8],
                                     OP.mult), VEs)
                pl.sop("vector", bTT(stat[:, 24:32], ssc[:, 0:8], stat[:, 16:24],
                                     OP.mult), VEs)
                pl.sop("vector", bRECIP(mksqr[:, 0:8], stat[:, 24:32]), VEs)
            pl.wait("vector", VEs, VEs.n)
            for t in range(TT):
                pl.op("vector", bTSS(mk_dst[:, t * FDIM:(t + 1) * FDIM],
                                     phi[:, t * FDIM:(t + 1) * FDIM],
                                     rinvc[:, t:t + 1], OP.mult), sem=VEs)
            return VEs.n

        def phase_barrier():
            for eng in Plan.ENGS:
                pl.wait(eng, PEs, PEs.n)
                pl.wait(eng, VEs, VEs.n)
                pl.wait(eng, SEs, SEs.n)

        wstate = {}

        def wfill(key, dst, src):
            st = wstate.get(key)
            if st is not None:
                pl.wait("gpsimd", st[0], st[1])
            s = get_bsem(key)
            v = pl.sdma("gpsimd", bDMA(dst, src), sem=s, k=16)
            return s, v

        def wused(key, val):
            wstate[key] = (PEs, val)

        def colsrc(w, chunk, cols=512):
            return w[:, chunk * cols:(chunk + 1) * cols].rearrange(
                "(kt p) c -> p kt c", p=128)

        def bufview(buf, cols=512):
            return buf.rearrange("p (kt c) -> p kt c", c=cols)

        pl.sdma("sync", bDMA(ident[:], dIdent), sem=dxs, k=16)
        pl.sdma("sync", bDMA(identf[:], dIdentF), sem=dxs, k=16)
        pl.sdma("sync", bDMA(iota16[:], dIota), sem=dxs, k=16)
        pl.sdma("sync", bDMA(rouT_sb.rearrange("p (kt e) -> p kt e", e=16),
                           dRouT.rearrange("(kt p) e -> p kt e", p=128)), sem=dxs, k=16)
        pl.sdma("sync", bDMA(norm_f[:], dNormPC), sem=dxs, k=16)
        pl.sdma("sync", bDMA(cumlt_sb[:], dCumLT), sem=dxs, k=16)
        pl.sdma("sync", bDMA(u8_sb[:], dU8), sem=dxs, k=16)
        pl.sdma("sync", bDMA(ec_sb[:], dEc), sem=dxs, k=16)
        pl.sdma("sync", bDMA(tokab_sb[:], dTokAB), sem=dxs, k=16)
        consts_x = dxs.n
        consts_x_box[0] = consts_x
        pl.sdma("gpsimd", bDMA(bk_row[0:1, :], dBk.rearrange("(o n) -> o n", o=1)), sem=dwC, k=16)
        pl.sdma("gpsimd", bDMA(bv_row[0:1, :], dBv.rearrange("(o n) -> o n", o=1)), sem=dwC, k=16)
        pl.sdma("gpsimd", bDMA(bq_row[0:1, :], dBq.rearrange("(o n) -> o n", o=1)), sem=dwC, k=16)
        pl.sdma("gpsimd", bDMA(bg_row[0:1, :], dBg.rearrange("(o n) -> o n", o=1)), sem=dwC, k=16)
        pl.sdma("gpsimd", bDMA(bmb_row[0:1, :], dBmb.rearrange("(o n) -> o n", o=1)), sem=dwC, k=16)
        pl.sdma("gpsimd", bDMA(norm_bf[:], dNormPC), sem=dwC, k=16)
        consts_w = dwC.n
        pl.op("vector", bMEMSET(onesb[:], 1.0), sem=VEs)
        pl.op("vector", bMEMSET(epsln_c[:], EPS_LN), sem=VEs)
        pl.op("vector", bMEMSET(onescol_bf[:], 1.0), sem=VEs)
        pl.op("vector", bMEMSET(bigi32[:], 8191), sem=VEs)
        pl.op("vector", bMEMSET(zrob[:], 0.0), sem=VEs)
        pl.op("vector", bMEMSET(zro8[:], 0.0), sem=VEs)
        ones_v = pl.op("vector", bMEMSET(onesf[:], 1.0), sem=VEs)
        initS = get_bsem("init")
        pl.wait("gpsimd", VEs, VEs.n)
        pl.op("gpsimd", bDMA(dTok.ap().rearrange("(p c) o -> p (c o)", p=128),
                             bigi32[:]), sem=initS, k=16)
        zro8v = zro8[:].rearrange("p (b c) -> p b c", c=2048)
        for bchunk in range(4):
            pl.sdma("gpsimd", bDMA(
                dBlob.ap().rearrange("(b p) c -> b p c", p=128)[
                    4 * bchunk:4 * bchunk + 4].transpose([1, 0, 2]),
                zro8v), sem=initS, k=16)
        for bchunk in range(8):
            pl.sdma("gpsimd", bDMA(
                dXs.ap().rearrange("(b p) c -> b p c", p=128)[
                    4 * bchunk:4 * bchunk + 4].transpose([1, 0, 2]),
                zro8v), sem=initS, k=16)
        zrof = carve(RD, 8192, 1024, F32)
        pl.sdma("gpsimd", bDMA(dWsl.ap().rearrange("(p c) o -> p (c o)", p=128),
                             zrof[:, 0:32]), sem=initS, k=16)
        wkS, wkV = wfill("rw0", bufview(carve(RW, 0, 8192, BF16)),
                         dWkT.rearrange("(kt p) c -> p kt c", p=128))
        wvS, wvV = wfill("rw1", bufview(carve(RW, 16384, 8192, BF16)),
                         dWvT.rearrange("(kt p) c -> p kt c", p=128))
        wk_buf = carve(RW, 0, 8192, BF16)
        wv_buf = carve(RW, 16384, 8192, BF16)

        pl.sdma("sync", bDMA(x_all.rearrange("p (t c) -> p t c", c=D),
                           dX.rearrange("(t p) c -> p t c", p=128)), sem=dxs, k=16)
        xload = dxs.n
        consts_x_box[0] = xload

        pl.wait("scalar", dxs, xload)
        for t in range(TT):
            xt = x_all[:, t * D:(t + 1) * D]
            pl.sop("scalar", bACT(dump0[:], xt, AF.Copy, accum=sx[:, t:t + 1]), SEs)
            pl.sop("scalar", bACT(dump0[:], xt, AF.Square, accum=sxx[:, t:t + 1]), SEs)
        stats_done = SEs.n
        pl.wait("vector", SEs, stats_done)
        pl.sop("vector", bTSS(stat[:, 0:8], sx[:, 0:8], 1.0 / D, OP.mult), VEs)
        pl.sop("vector", bTSS(stat[:, 8:16], sxx[:, 0:8], 1.0 / D, OP.mult), VEs)
        pl.sop("vector", bTT(stat[:, 16:24], stat[:, 0:8], stat[:, 0:8], OP.mult), VEs)
        var_v = pl.sop("vector", bTT(stat[:, 24:32], stat[:, 8:16], stat[:, 16:24],
                                     OP.subtract), VEs)
        pl.wait("scalar", VEs, var_v)
        pl.wait("scalar", VEs, ones_v)
        sq_v = pl.op("scalar", bACT(stat[:, 32:40], stat[:, 24:32], AF.Sqrt,
                                    bias=epsln_c[:, 0:1]), sem=SEs)
        pl.wait("vector", SEs, sq_v)
        pl.sop("vector", bRECIP(ln_rinv[:, 0:8], stat[:, 32:40]), VEs)
        ln_v = pl.sop("vector", bSTT(ln_nmr[:, 0:8], stat[:, 0:8], -1.0, ln_rinv[:, 0:8],
                                     OP.mult, OP.mult), VEs)

        router_pe = None
        for t in range(TT):
            xt = x_all[:, t * D:(t + 1) * D]
            pl.wait("vector", VEs, ln_v)
            pl.wait("vector", PEs, PEs.n)
            s_v = pl.sop("vector", bTS(xn32_scr[:], xt, stat[:, t:t + 1],
                                       ln_rinv[:, t:t + 1], OP.subtract, OP.mult), VEs)
            c_v = pl.sop("vector", bCOPY(xnbf_all[t][:], xn32_scr[:]), VEs)
            for half in range(2):
                srcs = [xnbf_all[t][:, (half * 8 + j) * 128:(half * 8 + j + 1) * 128]
                        for j in range(8)]
                dst = xn_fm.rearrange("p (f q) -> p f q", q=T)[
                    :, half * 8:(half + 1) * 8, t * 128:(t + 1) * 128]
                transpose_batch(srcs, dst, extra_wait=(VEs, c_v))
            for quad in range(4):
                b = next_bank()
                use_bank(b)
                pl.wait("tensor", VEs, s_v)
                pe_val = None
                for j in range(4):
                    jj = quad * 4 + j
                    pe_val = pl.op("tensor", bTR(b.h[:, j * 128:(j + 1) * 128],
                                                 xn32_scr[:, jj * 128:(jj + 1) * 128],
                                                 identf), sem=PEs)
                evac(b, bCOPY(xnfm32_t[:, quad * 512:(quad + 1) * 512], b.h[:]),
                     pe_val=pe_val)
            if t == 0:
                use_bank(pbS)
            pl.wait("tensor", dxs, xload)
            pl.wait("tensor", VEs, VEs.n)
            for kt in range(FT):
                router_pe = pl.op("tensor", bMM(pbS.h[:, t * 16:(t + 1) * 16],
                                                xnfm32_t[:, kt * 128:(kt + 1) * 128],
                                                rouT_sb[:, kt * 16:(kt + 1) * 16],
                                                kt == 0, kt == FT - 1), sem=PEs)
        evac(pbS, bCOPY(logit[:, 0:128], pbS.h[:, 0:128]), pe_val=router_pe)

        for t in range(TT):
            lt = logit[:, t * 16:(t + 1) * 16]
            p1c = wcols[:, t:t + 1]
            p2c = wcols[:, 8 + t:9 + t]
            i1c = stat[:, 40:41]
            m1 = m1_all[:, t * 16:(t + 1) * 16]
            m2 = m2_all[:, t * 16:(t + 1) * 16]
            pl.sop("vector", bRED(p1c, lt, OP.max), VEs)
            pl.sop("vector", bTSS(stat2[:, 0:16], lt, p1c, OP.is_ge), VEs)
            pl.sop("vector", bTSS(stat2[:, 16:32], iota16[:, 0:16], 1e4, OP.subtract), VEs)
            pl.sop("vector", bTT(stat2[:, 32:48], stat2[:, 16:32], stat2[:, 0:16],
                                 OP.mult), VEs)
            pl.sop("vector", bTSS(stat2[:, 48:64], stat2[:, 32:48], 1e4, OP.add), VEs)
            pl.sop("vector", bRED(i1c, stat2[:, 48:64], OP.min), VEs)
            pl.sop("vector", bTSS(stat2[:, 0:16], iota16[:, 0:16], i1c, OP.is_ge), VEs)
            pl.sop("vector", bTSS(stat2[:, 16:32], iota16[:, 0:16], i1c, OP.is_le), VEs)
            pl.sop("vector", bTT(m1, stat2[:, 0:16], stat2[:, 16:32], OP.mult), VEs)
            pl.sop("vector", bTSS(stat2[:, 32:48], m1, 1e6, OP.mult), VEs)
            pl.sop("vector", bTT(stat2[:, 48:64], lt, stat2[:, 32:48], OP.subtract), VEs)
            pl.sop("vector", bRED(p2c, stat2[:, 48:64], OP.max), VEs)
            pl.sop("vector", bTSS(stat2[:, 0:16], stat2[:, 48:64], p2c, OP.is_ge), VEs)
            pl.sop("vector", bTSS(stat2[:, 16:32], iota16[:, 0:16], 1e4, OP.subtract), VEs)
            pl.sop("vector", bTT(stat2[:, 32:48], stat2[:, 16:32], stat2[:, 0:16],
                                 OP.mult), VEs)
            pl.sop("vector", bTSS(stat2[:, 48:64], stat2[:, 32:48], 1e4, OP.add), VEs)
            pl.sop("vector", bRED(i1c, stat2[:, 48:64], OP.min), VEs)
            pl.sop("vector", bTSS(stat2[:, 0:16], iota16[:, 0:16], i1c, OP.is_ge), VEs)
            pl.sop("vector", bTSS(stat2[:, 16:32], iota16[:, 0:16], i1c, OP.is_le), VEs)
            pl.sop("vector", bTT(m2, stat2[:, 0:16], stat2[:, 16:32], OP.mult), VEs)
        dv = pl.sop("vector", bTT(wcols[:, 16:24], wcols[:, 0:8], wcols[:, 8:16],
                                  OP.subtract), VEs)
        pl.wait("scalar", VEs, dv)
        w1v = pl.op("scalar", bACT(wcols[:, 0:8], wcols[:, 16:24], AF.Sigmoid), sem=SEs)
        pl.wait("vector", SEs, w1v)
        pl.sop("vector", bTS(wcols[:, 8:16], wcols[:, 0:8], -1.0, 1.0, OP.mult, OP.add), VEs)
        for t in range(TT):
            pl.sop("vector", bTSS(stat2[:, 0:16], m1_all[:, t * 16:(t + 1) * 16],
                                  wcols[:, t:t + 1], OP.mult), VEs)
            pl.sop("vector", bTSS(stat2[:, 16:32], m2_all[:, t * 16:(t + 1) * 16],
                                  wcols[:, 8 + t:9 + t], OP.mult), VEs)
            pl.sop("vector", bTT(mask_tok[:, t * 16:(t + 1) * 16], stat2[:, 0:16],
                                 stat2[:, 16:32], OP.add), VEs)
        mask_done = VEs.n
        xn_done = VEs.n

        C = 256
        IOA = bass.IndirectOffsetOnAxis
        ohv = pl.sop("vector", bTT(oh_bf[:], m1_all[:], m2_all[:], OP.add), VEs)
        bC = next_bank()
        use_bank(bC)
        pl.wait("tensor", VEs, ohv)
        pl.wait("tensor", dxs, xload)
        pe_val = None
        for tt in range(TT):
            pe_val = pl.op("tensor", bMM(bC.h[:, tt * 16:(tt + 1) * 16], cumlt_sb[:],
                                         oh_bf[:, tt * 16:(tt + 1) * 16], True, True),
                           sem=PEs)
        evac(bC, bCOPY(ccum_f[:], bC.h[:, 0:128]), pe_val=pe_val)
        bT = next_bank()
        use_bank(bT)
        pl.wait("tensor", VEs, ones_v)
        pe_val = None
        for tt in range(TT):
            pe_val = pl.op("tensor", bMM(bT.h[0:16, tt:tt + 1],
                                         oh_bf[:, tt * 16:(tt + 1) * 16],
                                         onescol_bf[:], True, True), sem=PEs)
        evac(bT, bCOPY(tots_f[:], bT.h[0:16, 0:8]), pe_val=pe_val)
        bT2 = next_bank()
        use_bank(bT2)
        pl.wait("tensor", VEs, VEs.n)
        pe_val = pl.op("tensor", bTR(bT2.h[0:8, 0:16], tots_f[:], identf[0:16, 0:16]),
                       sem=PEs)
        evac(bT2, bCOPY(totsT_bf[:], bT2.h[0:8, 0:16]), pe_val=pe_val)
        bT3 = next_bank()
        use_bank(bT3)
        pl.wait("tensor", VEs, VEs.n)
        pe_val = pl.op("tensor", bMM(bT3.h[0:16, 0:8], totsT_bf[:], u8_sb[:],
                                     True, True), sem=PEs)
        evac(bT3, bCOPY(texE_f[:], bT3.h[0:16, 0:8]), pe_val=pe_val)
        bT4 = next_bank()
        use_bank(bT4)
        pl.wait("tensor", VEs, VEs.n)
        pe_val = pl.op("tensor", bTR(bT4.h[0:8, 0:16], texE_f[:], identf[0:16, 0:16]),
                       sem=PEs)
        evac(bT4, bCOPY(texT_f[:], bT4.h[0:8, 0:16]), pe_val=pe_val)
        pl.wait("sync", VEs, VEs.n)
        texS = get_bsem("texrow")
        tex_v = pl.op("sync", bDMA(texrow[0:1, :], texT_f[:]), sem=texS, k=16)
        bT5 = next_bank()
        use_bank(bT5)
        pl.wait("tensor", texS, tex_v)
        pe_val = pl.op("tensor", bMM(bT5.h[:, 0:128], onesf[0:1, :], texrow[0:1, :],
                                     True, True), sem=PEs)
        pl.wait("vector", PEs, pe_val)
        rk_v = pl.sop("vector", bTT(rank_f[:], ccum_f[:], bT5.h[:, 0:128], OP.add), VEs)
        bT5.free_sem, bT5.free_val = VEs, rk_v
        pl.sop("vector", bTT(logit[:], rank_f[:], ec_sb[:], OP.add), VEs)
        pl.sop("vector", bTSS(ccum_f[:], rank_f[:], float(C) - 0.5, OP.is_ge), VEs)
        pl.sop("vector", bTSS(rank_f[:], ccum_f[:], 16384.0, OP.mult), VEs)
        pl.sop("vector", bTT(ccum_f[:], logit[:], rank_f[:], OP.add), VEs)
        for mm_all, dst_f, dst_i in ((m1_all, destA_f, destA_i),
                                     (m2_all, destB_f, destB_i)):
            pl.sop("vector", bTT(logit[:], mm_all[:], ccum_f[:], OP.mult), VEs)
            for tt in range(TT):
                pl.sop("vector", bRED(dst_f[:, tt:tt + 1],
                                      logit[:, tt * 16:(tt + 1) * 16], OP.add), VEs)
            pl.sop("vector", bCOPY(dst_i[:], dst_f[:]), VEs)
        idx_v = VEs.n

        scatS = get_bsem("scat")
        pl.ops["gpsimd"].append(mk_reg("bc4095", 4095))
        pl.ops["gpsimd"].append(mk_reg("bc2047", 2047))
        pl.wait("gpsimd", VEs, idx_v)
        pl.wait("gpsimd", initS, initS.n)
        scatPool = [get_bsem("scp%d" % i) for i in range(6)]
        for tt in range(TT):
            pl.sdma("gpsimd", lambda E, o=dXs.ap(), i=destA_i[:, tt:tt + 1],
                  s=xnbf_all[tt][:]: E.indirect_dma_start(
                o, IOA(i, 0), s, None, bounds_check=regbox["bc4095"], oob_is_err=False),
                sem=scatPool[0], k=16)
            pl.sdma("gpsimd", lambda E, o=dXs.ap(), i=destB_i[:, tt:tt + 1],
                  s=xnbf_all[tt][:]: E.indirect_dma_start(
                o, IOA(i, 0), s, None, bounds_check=regbox["bc4095"], oob_is_err=False),
                sem=scatPool[1], k=16)
            pl.sdma("gpsimd", lambda E, o=dTok.ap(), i=destA_i[:, tt:tt + 1],
                  s=tokab_sb[:, tt:tt + 1]: E.indirect_dma_start(
                o, IOA(i, 0), s, None, bounds_check=regbox["bc4095"], oob_is_err=False),
                sem=scatPool[2], k=16)
            pl.sdma("gpsimd", lambda E, o=dTok.ap(), i=destB_i[:, tt:tt + 1],
                  s=tokab_sb[:, 8 + tt:9 + tt]: E.indirect_dma_start(
                o, IOA(i, 0), s, None, bounds_check=regbox["bc4095"], oob_is_err=False),
                sem=scatPool[3], k=16)
            pl.sdma("gpsimd", lambda E, o=dWsl.ap(), i=destA_i[:, tt:tt + 1],
                  s=wcols[:, tt:tt + 1]: E.indirect_dma_start(
                o, IOA(i, 0), s, None, bounds_check=regbox["bc4095"], oob_is_err=False),
                sem=scatPool[4], k=16)
            pl.sdma("gpsimd", lambda E, o=dWsl.ap(), i=destB_i[:, tt:tt + 1],
                  s=wcols[:, 8 + tt:9 + tt]: E.indirect_dma_start(
                o, IOA(i, 0), s, None, bounds_check=regbox["bc4095"], oob_is_err=False),
                sem=scatPool[5], k=16)
        for _s in scatPool:
            scatS.n = scatS.n
        scat_pool_vals = [(s, s.n) for s in scatPool]

        def wait_scatters(eng):
            for s, vv in scat_pool_vals:
                pl.wait(eng, s, vv)
        scat_all = 0

        phase_barrier()
        wait_scatters("gpsimd")
        pl.wait("gpsimd", SEs, SEs.n)
        memkh_v = pl.sdma("gpsimd", bDMA(mem_kh[:], dMemKH), sem=dwC, k=16)

        def proj_tok(wbuf, wsem, wval, bias_row, out_emit):
            for t in range(TT):
                b = next_bank()
                use_bank(b)
                pl.wait("tensor", wsem, wval)
                pl.wait("tensor", VEs, xn_done)
                pl.wait("tensor", dwC, dwC.n)
                pe_val = None
                for kt in range(FT):
                    pe_val = pl.op("tensor", bMM(
                        b.h[:],
                        xn_fm[:, kt * T + t * 128:kt * T + (t + 1) * 128],
                        wbuf[:, kt * 512:(kt + 1) * 512],
                        kt == 0, False), sem=PEs)
                pe_val = pl.op("tensor", bMM(b.h[:], onesb[0:1, :], bias_row,
                                             False, True), sem=PEs)
                out_emit(t, b, pe_val)

        def k_emit(t, b, pe_val):
            evac(b, bCOPY(k4_bf[:, t * 512:(t + 1) * 512], b.h[:]), pe_val=pe_val)

        proj_tok(wk_buf, wkS, wkV, bk_row[0:1, 0:512], k_emit)
        wused("rw0", PEs.n)

        def v_emit(t, b, pe_val):
            evac(b, bCOPY(v4_bf[:, t * 512:(t + 1) * 512], b.h[:]), pe_val=pe_val)

        wait_scatters("vector")
        wait_scatters("scalar")
        proj_tok(wv_buf, wvS, wvV, bv_row[0:1, 0:512], v_emit)
        wused("rw1", PEs.n)

        prev_kv_pe = 0
        prev_kv_ve = 0
        prev_kv_se = 0
        stage_cnt = 0
        for kv in range(HKV):
            pl.wait("vector", PEs, prev_kv_pe)
            pl.wait("vector", SEs, prev_kv_se)
            pl.wait("scalar", VEs, prev_kv_ve)
            pl.wait("scalar", PEs, prev_kv_pe)
            rkey = "rw%d" % (kv % 2)
            wmb_buf = carve(RW, (kv % 2) * 16384, 8192, BF16)
            wmbS, wmbV = wfill(rkey, bufview(wmb_buf), colsrc(dWmbT, kv))

            def mb_emit(t, b, pe_val):
                evac(b, bACT(mb4_bf[:, t * 512:(t + 1) * 512], b.h[:], AF.Sigmoid),
                     eng="scalar", pe_val=pe_val)

            proj_tok(wmb_buf, wmbS, wmbV, bmb_row[0:1, kv * 512:(kv + 1) * 512], mb_emit)
            wused(rkey, PEs.n)

            mk_v = emit_dpfp(
                lambda t, kv=kv: k4_bf[:, t * 512 + kv * 128:t * 512 + (kv + 1) * 128],
                a1_all, mk_tok, dump1[:], True)
            for k in range(FK):
                srcs = [mk_tok[:, t * FDIM + k * 128:t * FDIM + (k + 1) * 128]
                        for t in range(TT)]
                transpose_batch(srcs, mk_fm[:, k * T:(k + 1) * T], extra_wait=(VEs, mk_v))
            mkfm_v = VEs.n

            use_bank(pbS)
            norm4 = norm_bf.rearrange("p (h k) -> p h k", k=FK)
            pe_val = None
            pl.wait("tensor", VEs, mkfm_v)
            pl.wait("tensor", dwC, dwC.n)
            for t in range(TT):
                for k in range(FK):
                    pe_val = pl.op("tensor", bMM(
                        pbS.h[:, t * 4:(t + 1) * 4],
                        mk_fm[:, k * T + t * 128:k * T + (t + 1) * 128],
                        norm4[:, kv * 4:(kv + 1) * 4, k],
                        k == 0, k == FK - 1), sem=PEs)
            evac(pbS, bTSS(den4[:, 0:32], pbS.h[:, 0:32], EPS_A, OP.add), pe_val=pe_val)
            pl.sop("vector", bRECIP(drec4[:, 0:32], den4[:, 0:32]), VEs)
            den4v = den4.rearrange("p (t r) -> p t r", r=4)
            coef4v = coef4.rearrange("p (t r) -> p t r", r=4)
            for r in range(4):
                pl.sop("vector", bTT(stat[:, 0:8], den4v[:, :, r], mksqr[:, 0:8],
                                     OP.mult), VEs)
                pl.sop("vector", bTS(stat[:, 8:16], stat[:, 0:8], -1.0, 1.0,
                                     OP.mult, OP.add), VEs)
                pl.sop("vector", bTS(coef4v[:, :, r], stat[:, 8:16], 0.0, 1.0,
                                     OP.max, OP.min), VEs)
            coef_v = VEs.n

            for t in range(TT):
                b = next_bank()
                use_bank(b)
                pl.wait("tensor", dwC, memkh_v)
                pe_val = None
                for k in range(FK):
                    pe_val = pl.op("tensor", bMM(
                        b.h[:],
                        mk_fm[:, k * T + t * 128:k * T + (t + 1) * 128],
                        mem_kh[:, (k * 16 + kv * 4) * 128:(k * 16 + kv * 4 + 4) * 128],
                        k == 0, k == FK - 1), sem=PEs)
                pl.wait("vector", PEs, pe_val)
                pl.wait("vector", VEs, coef_v)
                last_v = None
                for r in range(4):
                    pl.sop("vector", bTSS(pv_scr[:], b.h[:, r * 128:(r + 1) * 128],
                                          drec4[:, t * 4 + r:t * 4 + r + 1], OP.mult), VEs)
                    pl.sop("vector", bTT(mv_scr[:],
                                         v4_bf[:, t * 512 + kv * 128:t * 512 + (kv + 1) * 128],
                                         pv_scr[:], OP.subtract), VEs)
                    last_v = pl.sop("vector", bTT(
                        wmv4[:, t * 512 + r * 128:t * 512 + (r + 1) * 128], mv_scr[:],
                        mb4_bf[:, t * 512 + r * 128:t * 512 + (r + 1) * 128],
                        OP.mult), VEs)
                b.free_sem, b.free_val = VEs, last_v
            wmv_v = VEs.n

            for r in range(4):
                h = kv * 4 + r
                for rnd in range(2):
                    use_bank(pbD)
                    pl.wait("tensor", VEs, wmv_v)
                    pe_val = None
                    for kk in range(3):
                        k = rnd * 3 + kk
                        for t in range(TT):
                            pe_val = pl.op("tensor", bMM(
                                pbD.h[:, kk * 128:(kk + 1) * 128],
                                mk_tok[:, t * FDIM + k * 128:t * FDIM + (k + 1) * 128],
                                wmv4[:, t * 512 + r * 128:t * 512 + (r + 1) * 128],
                                t == 0, t == TT - 1), sem=PEs)
                        for t in range(TT):
                            pe_val = pl.op("tensor", bMM(
                                pbD.h[:, 384 + kk:385 + kk],
                                mk_tok[:, t * FDIM + k * 128:t * FDIM + (k + 1) * 128],
                                coef4[:, t * 4 + r:t * 4 + r + 1],
                                t == 0, t == TT - 1), sem=PEs)
                    pl.wait("vector", dxs, stage_cnt)
                    pl.wait("vector", PEs, pe_val)
                    pl.op("vector", bCOPY(stg_m[:, rnd * 384:(rnd + 1) * 384],
                                          pbD.h[:, 0:384]), sem=VEs)
                    v2 = pl.op("vector", bCOPY(stg_n[:, rnd * 3:(rnd + 1) * 3],
                                               pbD.h[:, 384:387]), sem=VEs)
                    pbD.free_sem, pbD.free_val = VEs, v2
                pl.wait("sync", VEs, VEs.n)
                pl.sdma("sync", bDMA(dCCmi.ap()[:, h * 768:(h + 1) * 768], stg_m[:]),
                      sem=dxs, k=16)
                pl.sdma("sync", bDMA(dCCni.ap()[:, h * 6:(h + 1) * 6], stg_n[:]),
                      sem=dxs, k=16)
                stage_cnt = dxs.n
            prev_kv_pe = PEs.n
            prev_kv_ve = VEs.n
            prev_kv_se = SEs.n

        pl.wait("gpsimd", dxs, stage_cnt)
        pl.op("gpsimd", lambda Eng: Eng.collective_compute(
            "AllReduce", OP.add, replica_groups=[list(range(NCORES))],
            ins=[dCCmi.ap().opt()], outs=[dCCmo.ap().opt()]), sem=csem)
        pl.op("gpsimd", lambda Eng: Eng.collective_compute(
            "AllReduce", OP.add, replica_groups=[list(range(NCORES))],
            ins=[dCCni.ap().opt()], outs=[dCCno.ap().opt()]), sem=csem)

        phase_barrier()
        ffn_acc = carve(RX, 0, 16384, F32)
        gq = [carve(RW, 0, 4096, BF16), carve(RW, 8192, 4096, BF16)]
        uq = [carve(RW, 16384, 4096, BF16), carve(RW, 24576, 4096, BF16)]
        h_all = carve(RH, 0, 8192, BF16)
        qctr = [0]
        prev_down_pe = [0]

        for ui in range(2):
            pl.wait("vector", PEs, prev_down_pe[0])
            for qi in range(4):
                par = qctr[0] % 2
                qctr[0] += 1
                gsrc = colsrc(dSgw, ui * 4 + qi, cols=256)
                usrc = colsrc(dSuw, ui * 4 + qi, cols=256)
                gS, gv = wfill("gq%d" % par, bufview(gq[par], cols=256), gsrc)
                uS, uv = wfill("uq%d" % par, bufview(uq[par], cols=256), usrc)
                for lhf in range(2):
                    hf = qi * 2 + lhf
                    for ch in range(2):
                        bG = next_bank()
                        bU = next_bank()
                        use_bank(bG)
                        use_bank(bU)
                        pl.wait("tensor", gS, gv)
                        pl.wait("tensor", uS, uv)
                        pl.wait("tensor", VEs, xn_done)
                        for kt in range(FT):
                            pl.op("tensor", bMM(
                                bG.h[:],
                                gq[par][:, kt * 256 + lhf * 128:kt * 256 + (lhf + 1) * 128],
                                xn_fm[:, kt * T + ch * 512:kt * T + (ch + 1) * 512],
                                kt == 0, kt == FT - 1), sem=PEs)
                            pe_val = pl.op("tensor", bMM(
                                bU.h[:],
                                uq[par][:, kt * 256 + lhf * 128:kt * 256 + (lhf + 1) * 128],
                                xn_fm[:, kt * T + ch * 512:kt * T + (ch + 1) * 512],
                                kt == 0, kt == FT - 1), sem=PEs)
                        pl.wait("scalar", PEs, pe_val)
                        pl.wait("scalar", VEs, VEs.n)
                        sv = pl.op("scalar", bACT(sil_scr[:], bG.h[:], AF.Silu), sem=SEs)
                        bG.free_sem, bG.free_val = SEs, sv
                        pl.wait("vector", SEs, sv)
                        hcol = hf * T + ch * 512
                        hv = pl.sop("vector", bTT(h_all[:, hcol:hcol + 512],
                                                  sil_scr[:], bU.h[:], OP.mult), VEs)
                        bU.free_sem, bU.free_val = VEs, hv
                wused("gq%d" % par, PEs.n)
                wused("uq%d" % par, PEs.n)
            h_done = VEs.n

            for oc in range(4):
                dpar = oc % 2
                dsrc = dSdw[ui * FE:(ui + 1) * FE,
                            oc * 512:(oc + 1) * 512].rearrange(
                    "(kt p) c -> p kt c", p=128)
                dS, dnv = wfill("dn%d" % dpar, bufview(dn_buf[dpar]), dsrc)
                for t in range(TT):
                    b = next_bank()
                    use_bank(b)
                    pl.wait("tensor", dS, dnv)
                    pl.wait("tensor", VEs, h_done)
                    pe_val = None
                    for kt in range(8):
                        pe_val = pl.op("tensor", bMM(
                            b.h[:],
                            h_all[:, kt * T + t * 128:kt * T + (t + 1) * 128],
                            dn_buf[dpar][:, kt * 512:(kt + 1) * 512],
                            kt == 0, kt == 7), sem=PEs)
                    aslice = ffn_acc[:, t * D + oc * 512:t * D + (oc + 1) * 512]
                    if ui == 0:
                        evac(b, bCOPY(aslice, b.h[:]), pe_val=pe_val)
                    else:
                        evac(b, bTT(aslice, aslice, b.h[:], OP.add), pe_val=pe_val)
                wused("dn%d" % dpar, PEs.n)
            prev_down_pe[0] = PEs.n
        pl.wait("sync", VEs, VEs.n)
        accoutS = get_bsem("accout")
        accout_v = pl.op("sync", bDMA(dAcc.ap(), ffn_acc[:]), sem=accoutS, k=16)

        xs_sb = carve(RH, 4096, 2048, BF16)
        xs_fm = carve(RH, 8192, 4096, BF16)
        h_e = carve(RH, 0, 2048, BF16)
        dn_sb = [[carve(RS, ep * 8192 + st * 4096, 2048, BF16) for st in range(2)]
                 for ep in range(2)]
        xsbS = get_bsem("xsb")
        tkS = [get_bsem("tk%d" % i) for i in range(4)]
        wsS2 = [get_bsem("ws%d" % i) for i in range(4)]
        scatBp = [get_bsem("scatb0"), get_bsem("scatb1")]
        xsb_pe = [prev_down_pe[0]]
        ring_last_scat = [0, 0, 0, 0]
        dnsb_last_scat = [0, 0]
        pend = [None]

        mark_i = [0]

        def flush_scatters():
            if pend[0] is None:
                return
            epar, info, dn_done = pend[0]
            pend[0] = None
            pl.wait("gpsimd", VEs, dn_done)
            for st, r, tv in info:
                sdud = get_bsem("scatdud%d" % st)
                pl.wait("gpsimd", tkS[r % 4], tv)
                pl.op("gpsimd", lambda E, o=dBlob.ap(), i=tok_col[:, r:r + 1],
                      s=dn_sb[epar][st][:]: E.indirect_dma_start(
                    o, IOA(i, 0), s, None, bounds_check=regbox["bc2047"],
                    oob_is_err=False), sem=sdud, k=16)
            for st in range(2):
                sd = get_bsem("scatdud%d" % st)
                pl.wait("gpsimd", sd, sd.n)
            pl.wait("gpsimd", scatBp[epar], dnsb_last_scat[epar])
            v = pl.op("gpsimd", bDMA(dDummy.ap()[:, epar * 64:(epar + 1) * 64],
                                     carve(RS, epar * 8192, 8192, BF16)[:, 0:64]),
                      sem=scatBp[epar], k=16)
            dnsb_last_scat[epar] = v

        def emit_gufill(e, qi):
            par = qctr[0] % 2
            qctr[0] += 1
            gS, gv = wfill("gq%d" % par, bufview(gq[par], cols=256),
                           colsrc(dGateW[e], qi, cols=256))
            uS, uv = wfill("uq%d" % par, bufview(uq[par], cols=256),
                           colsrc(dUpW[e], qi, cols=256))
            return (par, gS, gv, uS, uv)

        for e in range(E):
            epar = e % 2
            fills = {0: emit_gufill(e, 0), 1: emit_gufill(e, 1)}

            info = []
            for st in range(2):
                wait_scatters("sync")
                pl.wait("sync", PEs, xsb_pe[0])
                xv = pl.op("sync", bDMA(
                    xs_sb[:],
                    dXs.ap().rearrange("(s p) c -> s p c", p=128)[e * 2 + st]),
                    sem=xsbS, k=16)
                for half in range(2):
                    srcs = [xs_sb[:, (half * 8 + j) * 128:(half * 8 + j + 1) * 128]
                            for j in range(8)]
                    dst = xs_fm.rearrange("p (k c) -> p k c", c=256)[
                        :, half * 8:(half + 1) * 8, st * 128:(st + 1) * 128]
                    transpose_batch(srcs, dst, extra_wait=(xsbS, xv))
                xsb_pe[0] = PEs.n
                r = e * 2 + st
                rr4 = r % 4
                tv = pl.sdma("sync", bDMA(
                    tok_col[:, r:r + 1],
                    dTok.ap().rearrange("(s p) o -> s p o", p=128)[e * 2 + st]),
                    tkS[rr4], k=16)
                wv2 = pl.sdma("sync", bDMA(
                    wsl_col[:, r:r + 1],
                    dWsl.ap().rearrange("(s p) o -> s p o", p=128)[e * 2 + st]),
                    wsS2[rr4], k=16)
                info.append((st, r, tv, wv2))
            xsfm_v = VEs.n

            pl.wait("vector", PEs, prev_down_pe[0])
            for qi in range(4):
                par, gS, gv, uS, uv = fills[qi]
                del fills[qi]
                for lhf in range(2):
                    hf = qi * 2 + lhf
                    bG = next_bank()
                    bU = next_bank()
                    use_bank(bG)
                    use_bank(bU)
                    pl.wait("tensor", gS, gv)
                    pl.wait("tensor", uS, uv)
                    pl.wait("tensor", VEs, xsfm_v)
                    for kt in range(FT):
                        pl.op("tensor", bMM(
                            bG.h[:, 0:256],
                            gq[par][:, kt * 256 + lhf * 128:kt * 256 + (lhf + 1) * 128],
                            xs_fm[:, kt * 256:(kt + 1) * 256],
                            kt == 0, kt == FT - 1), sem=PEs)
                        pe_val = pl.op("tensor", bMM(
                            bU.h[:, 0:256],
                            uq[par][:, kt * 256 + lhf * 128:kt * 256 + (lhf + 1) * 128],
                            xs_fm[:, kt * 256:(kt + 1) * 256],
                            kt == 0, kt == FT - 1), sem=PEs)
                    pl.wait("scalar", PEs, pe_val)
                    pl.wait("scalar", VEs, VEs.n)
                    sv = pl.op("scalar", bACT(sil_scr[:, 0:256], bG.h[:, 0:256],
                                              AF.Silu), sem=SEs)
                    bG.free_sem, bG.free_val = SEs, sv
                    pl.wait("vector", SEs, sv)
                    hv = pl.sop("vector", bTT(h_e[:, hf * 256:(hf + 1) * 256],
                                              sil_scr[:, 0:256], bU.h[:, 0:256],
                                              OP.mult), VEs)
                    bU.free_sem, bU.free_val = VEs, hv
                wused("gq%d" % par, PEs.n)
                wused("uq%d" % par, PEs.n)
                if qi + 2 < 4:
                    fills[qi + 2] = emit_gufill(e, qi + 2)
                if qi == 0:
                    flush_scatters()
            h_done = VEs.n

            for oc in range(4):
                dpar = oc % 2
                dsrc = dDownW[e][:, oc * 512:(oc + 1) * 512].rearrange(
                    "(kt p) c -> p kt c", p=128)
                dS, dnv = wfill("dn%d" % dpar, bufview(dn_buf[dpar]), dsrc)
                for st, r, tv, wv2 in info:
                    b = next_bank()
                    use_bank(b)
                    pl.wait("tensor", dS, dnv)
                    pl.wait("tensor", VEs, h_done)
                    pe_val = None
                    for kt in range(8):
                        pe_val = pl.op("tensor", bMM(
                            b.h[:],
                            h_e[:, kt * 256 + st * 128:kt * 256 + (st + 1) * 128],
                            dn_buf[dpar][:, kt * 512:(kt + 1) * 512],
                            kt == 0, kt == 7), sem=PEs)
                    pl.wait("vector", wsS2[r % 4], wv2)
                    pl.wait("vector", scatBp[epar], dnsb_last_scat[epar])
                    evac(b, lambda E, o=dn_sb[epar][st][:, oc * 512:(oc + 1) * 512],
                         i=b.h[:], w=wsl_col[:, r:r + 1]: E.tensor_single_scalar(
                        o, i, w, OP.mult), pe_val=pe_val)
                wused("dn%d" % dpar, PEs.n)
            pend[0] = (epar, [(st, r, tv) for st, r, tv, _ in info], VEs.n)
            prev_down_pe[0] = PEs.n
        flush_scatters()
        scatb_all = [(s, s.n) for s in scatBp]

        phase_barrier()
        pl.wait("vector", accoutS, accout_v)
        pl.wait("scalar", accoutS, accout_v)
        for s, vv in scatb_all:
            pl.wait("vector", s, vv)
            pl.wait("scalar", s, vv)
        pl.wait("gpsimd", csem, 2)
        nn_dma = pl.sdma("gpsimd", bDMA(ccn_sb[:], dCCno.ap()), sem=dwC, k=16)
        pl.wait("vector", dwC, nn_dma)
        pl.wait("vector", dxs, dxs.n)
        nn_v = pl.sop("vector", bTT(nn_bf[:], ccn_sb[:], norm_f[:], OP.add), VEs)

        newm2 = [newm_bf, carve(RD, 4608, 768, BF16)]
        sS = get_bsem("s_dma")
        last_newm = [0]
        last_s = [0]
        phi_cons = [0, 0]
        newm_cons = [0, 0]

        def q_emit(t, b, pe_val):
            evac(b, bCOPY(q4_bf[:, t * 512:(t + 1) * 512], b.h[:]), pe_val=pe_val)

        def g_emit(t, b, pe_val):
            evac(b, bACT(g4_bf[:, t * 512:(t + 1) * 512], b.h[:], AF.Sigmoid),
                 eng="scalar", pe_val=pe_val)

        def head_stageA(h, par):
            r = h % 4
            pl.wait("gpsimd", csem, 2)
            pl.wait("gpsimd", VEs, last_newm[0])
            pl.sdma("gpsimd", bDMA(ccm_sb[:], dCCmo.ap()[:, h * 768:(h + 1) * 768]),
                  sem=dwC, k=16)
            mhd_v = pl.sdma("gpsimd", bDMA(memh_sb[:], dMemHC[:, h * 768:(h + 1) * 768]),
                          sem=dwC, k=16)
            pl.wait("vector", dwC, mhd_v)
            pl.wait("vector", PEs, newm_cons[par])
            newm_v = pl.sop("vector", bTT(newm2[par][:], ccm_sb[:], memh_sb[:],
                                          OP.add), VEs)
            last_newm[0] = newm_v
            pl.wait("vector", PEs, phi_cons[par])
            mq_v = emit_dpfp(
                lambda t, r=r: q4_bf[:, t * 512 + r * 128:t * 512 + (r + 1) * 128],
                a4_all, None, dump4[:], False, phi_ap=phi4[par])
            return (mq_v, newm_v)

        def head_stageB(h, par, mq_v, newm_v):
            r = h % 4
            for k in range(FK):
                srcs = [phi4[par][:, t * FDIM + k * 128:t * FDIM + (k + 1) * 128]
                        for t in range(TT)]
                transpose_batch(srcs, mq_fm2[par][:, k * T:(k + 1) * T],
                                extra_wait=(VEs, mq_v))
            phi_cons[par] = PEs.n
            mqfm_v = VEs.n
            use_bank(pbS)
            pl.wait("tensor", VEs, max(mqfm_v, nn_v))
            pe_val = None
            for t in range(TT):
                for k in range(FK):
                    pe_val = pl.op("tensor", bMM(
                        pbS.h[:, t:t + 1],
                        mq_fm2[par][:, k * T + t * 128:k * T + (t + 1) * 128],
                        nn_bf[:, h * 6 + k:h * 6 + k + 1],
                        k == 0, k == FK - 1), sem=PEs)
            evac(pbS, bTSS(denac[:, 0:8], pbS.h[:, 0:8], EPS_A, OP.add),
                 pe_val=pe_val)
            pl.sop("vector", bRECIP(darec[:, 0:8], denac[:, 0:8]), VEs)
            darec_v = VEs.n
            srcs = [xn_fm[:, h * T + t * 128:h * T + (t + 1) * 128]
                    for t in range(TT)]
            xnt_v = transpose_batch(srcs, xnt_h[:])
            pl.wait("vector", sS, last_s[0])
            for t in range(TT):
                b = next_bank()
                use_bank(b)
                pl.wait("tensor", VEs, newm_v)
                pe_val = None
                for k in range(FK):
                    pe_val = pl.op("tensor", bMM(
                        b.h[:, 0:128],
                        mq_fm2[par][:, k * T + t * 128:k * T + (t + 1) * 128],
                        newm2[par][:, k * 128:(k + 1) * 128],
                        k == 0, k == FK - 1), sem=PEs)
                pl.wait("vector", PEs, pe_val)
                pl.wait("vector", VEs, max(darec_v, xnt_v))
                pl.sop("vector", bTSS(mh_scr[:], b.h[:, 0:128],
                                      darec[:, t:t + 1], OP.mult), VEs)
                pl.sop("vector", bTT(
                    sg_scr[:],
                    g4_bf[:, t * 512 + r * 128:t * 512 + (r + 1) * 128],
                    mh_scr[:], OP.mult), VEs)
                sv = pl.sop("vector", bTT(
                    s_h[:, t * 128:(t + 1) * 128],
                    xnt_h[:, t * 128:(t + 1) * 128], sg_scr[:], OP.add), VEs)
                b.free_sem, b.free_val = VEs, sv
            newm_cons[par] = PEs.n
            pl.wait("sync", VEs, VEs.n)
            pl.wait("sync", sS, sS.n)
            last_s[0] = pl.op("sync", bDMA(
                dSd.ap().rearrange("p (t g c) -> p t g c", g=16, c=128)[:, :, h, :],
                s_h[:].rearrange("p (t c) -> p t c", c=128)), sem=sS, k=16)

        pending = [None]
        for h in range(16):
            if h % 4 == 0:
                if pending[0] is not None:
                    head_stageB(*pending[0])
                    pending[0] = None
                kvg = h // 4
                pl.wait("vector", PEs, PEs.n)
                pl.wait("scalar", PEs, PEs.n)
                pl.wait("scalar", VEs, VEs.n)
                pl.wait("vector", SEs, SEs.n)
                wqS, wq_v = wfill("rw0", bufview(wq_buf), colsrc(dWqT, kvg))
                wgS, wg_v = wfill("rw1", bufview(wg_buf), colsrc(dWgT, kvg))
                proj_tok(wq_buf, wqS, wq_v, bq_row[0:1, kvg * 512:(kvg + 1) * 512],
                         q_emit)
                wused("rw0", PEs.n)
                proj_tok(wg_buf, wgS, wg_v, bg_row[0:1, kvg * 512:(kvg + 1) * 512],
                         g_emit)
                wused("rw1", PEs.n)
            mq_v, newm_v = head_stageA(h, h % 2)
            if pending[0] is not None:
                head_stageB(*pending[0])
            pending[0] = (h, h % 2, mq_v, newm_v)
        head_stageB(*pending[0])
        s_all = sS.n
        s_done = VEs.n

        phase_barrier()
        pl.wait("gpsimd", accoutS, accout_v)
        pl.wait("gpsimd", PEs, PEs.n)
        pl.wait("sync", PEs, PEs.n)
        for s, vv in scatb_all:
            pl.wait("sync", s, vv)
        pl.wait("sync", sS, s_all)
        s5 = [carve(RX, 49152, 2048, BF16), carve(RX, 53248, 2048, BF16)]
        s5S = [get_bsem("s5_%d" % i) for i in range(2)]
        accTS = [get_bsem("a5_%d" % i) for i in range(2)]
        x5S = [get_bsem("x5_%d" % i) for i in range(2)]
        outS = [get_bsem("o5_%d" % i) for i in range(2)]
        blAS = [get_bsem("bA_%d" % i) for i in range(2)]
        blBS = [get_bsem("bB_%d" % i) for i in range(2)]
        blA = [carve(RW, 0, 2048, BF16), carve(RW, 4096, 2048, BF16)]
        blB = [carve(RW, 8192, 2048, BF16), carve(RW, 12288, 2048, BF16)]
        fsum = carve(RS, 16384, 2048, F32)
        blobv = dBlob.ap().rearrange("(b p) c -> b p c", p=128)
        prev_ve_of = {}
        for t in range(TT):
            par = t % 2
            if t >= 2:
                pl.wait("gpsimd", VEs, prev_ve_of[t - 2])
                pl.wait("sync", VEs, prev_ve_of[t - 2])
            ac_v = pl.op("gpsimd", bDMA(acc_t[par][:], dAcc.ap()[:, t * D:(t + 1) * D]),
                         sem=accTS[par], k=16)
            bA_v = pl.op("sync", bDMA(blA[par][:], blobv[t]), sem=blAS[par], k=16)
            sl_v = pl.op("sync", bDMA(s5[par][:], dSd.ap()[:, t * D:(t + 1) * D]),
                         sem=s5S[par], k=16)
            bB_v = pl.op("sync", bDMA(blB[par][:], blobv[8 + t]), sem=blBS[par], k=16)
            x_v = pl.op("sync", bDMA(x5_t[par][:],
                                     dX.rearrange("(t p) c -> t p c", p=128)[t]),
                        sem=x5S[par], k=16)
            pl.wait("vector", accTS[par], ac_v)
            pl.wait("vector", blAS[par], bA_v)
            pl.wait("vector", blBS[par], bB_v)
            if t >= 2:
                pl.wait("vector", outS[par], 16 * (t // 2))
            pl.sop("vector", bTT(fsum[:], acc_t[par][:], blA[par][:], OP.add), VEs)
            f2 = pl.sop("vector", bTT(t2_scr[:], fsum[:], blB[par][:], OP.add), VEs)
            pl.wait("scalar", VEs, f2)
            si_v = pl.op("scalar", bACT(silu5[par][:], t2_scr[:], AF.Silu), sem=SEs)
            pl.wait("vector", SEs, si_v)
            pl.wait("vector", VEs, s_done)
            pl.wait("vector", x5S[par], x_v)
            pl.wait("vector", s5S[par], sl_v)
            pl.sop("vector", bTT(t2_scr[:], s5[par][:], silu5[par][:],
                                 OP.mult), VEs)
            ov = pl.sop("vector", bTT(out_st[par][:], x5_t[par][:], t2_scr[:], OP.add),
                        VEs)
            prev_ve_of[t] = ov
            pl.wait("sync", VEs, ov)
            pl.op("sync", bDMA(dOut.rearrange("(t p) c -> t p c", p=128)[t],
                               out_st[par][:]), sem=outS[par], k=16)
        pl.wait("sync", outS[0], 16 * 4)
        pl.wait("sync", outS[1], 16 * 4)

        if DEBUG:
            dbgS = get_bsem("dbg")
            o_dA = nc.dram_tensor("o_dA", [128, 8], I32, kind="ExternalOutput").ap()
            o_dB = nc.dram_tensor("o_dB", [128, 8], I32, kind="ExternalOutput").ap()
            o_w = nc.dram_tensor("o_w", [128, 32], F32, kind="ExternalOutput").ap()
            o_m = nc.dram_tensor("o_m", [128, 128], F32, kind="ExternalOutput").ap()
            o_tok = nc.dram_tensor("o_tok", [4096, 1], I32, kind="ExternalOutput").ap()
            o_wsl = nc.dram_tensor("o_wsl", [4096, 1], F32, kind="ExternalOutput").ap()
            o_blob = nc.dram_tensor("o_blob", [2048, 2048], BF16,
                                    kind="ExternalOutput").ap()
            for s, vv in scatb_all:
                pl.wait("sync", s, vv)
            pl.op("sync", bDMA(o_dA, destA_i[:]), sem=dbgS, k=16)
            pl.op("sync", bDMA(o_dB, destB_i[:]), sem=dbgS, k=16)
            pl.op("sync", bDMA(o_w, wcols[:]), sem=dbgS, k=16)
            pl.op("sync", bDMA(o_m, mask_tok[:]), sem=dbgS, k=16)
            pl.op("sync", bDMA(o_tok, dTok.ap()), sem=dbgS, k=16)
            pl.op("sync", bDMA(o_wsl, dWsl.ap()), sem=dbgS, k=16)
            pl.op("sync", bDMA(o_blob, dBlob.ap()), sem=dbgS, k=16)
            o_xs = nc.dram_tensor("o_xs", [4096, 2048], BF16, kind="ExternalOutput").ap()
            pl.op("sync", bDMA(o_xs, dXs.ap()), sem=dbgS, k=16)
            pl.wait("sync", dbgS, dbgS.n)

        with nc.Block() as block:

            @block.tensor
            def _(Eng):
                for f in pl.ops["tensor"]:
                    f(Eng)

            @block.vector
            def _(Eng):
                for f in pl.ops["vector"]:
                    f(Eng)

            @block.scalar
            def _(Eng):
                for f in pl.ops["scalar"]:
                    f(Eng)

            @block.gpsimd
            def _(Eng):
                for f in pl.ops["gpsimd"]:
                    f(Eng)

            @block.sync
            def _(Eng):
                for f in pl.ops["sync"]:
                    f(Eng)

    return nc


_NC_CACHE = {}


def kernel(L0_attn_hidden_states, L0_attn_Wq, L0_attn_bq, L0_attn_Wk, L0_attn_bk,
           L0_attn_Wv, L0_attn_bv, L0_attn_Wg, L0_attn_bg, L0_attn_Wmb, L0_attn_bmb,
           L0_attn_memory, L0_attn_norm, L0_ffn_router_w, L0_ffn_gate_w, L0_ffn_up_w,
           L0_ffn_down_w, L0_ffn_shared_gate_w, L0_ffn_shared_up_w,
           L0_ffn_shared_down_w):
    f32 = np.float32
    x = np.ascontiguousarray(np.asarray(L0_attn_hidden_states, f32).reshape(8192, D))
    wqT = np.ascontiguousarray(np.asarray(L0_attn_Wq, f32).T)
    wkT = np.ascontiguousarray(np.asarray(L0_attn_Wk, f32).T)
    wvT = np.ascontiguousarray(np.asarray(L0_attn_Wv, f32).T)
    wgT = np.ascontiguousarray(np.asarray(L0_attn_Wg, f32).T)
    wmbT = np.ascontiguousarray(np.asarray(L0_attn_Wmb, f32).T)
    mem = np.asarray(L0_attn_memory, f32)
    mem_kh = np.ascontiguousarray(
        mem.reshape(16, 6, 128, 128).transpose(2, 1, 0, 3).reshape(128, 12288))
    mem_hc = np.ascontiguousarray(
        mem.reshape(16, 6, 128, 128).transpose(2, 0, 1, 3).reshape(128, 12288))
    norm_pc = np.ascontiguousarray(
        np.asarray(L0_attn_norm, f32).reshape(16, 6, 128).transpose(2, 0, 1).reshape(128, 96))
    routerT = np.ascontiguousarray(np.asarray(L0_ffn_router_w, f32).T)
    ident = np.eye(128, dtype=BF)
    identf = np.eye(128, dtype=f32)
    iota16 = np.ascontiguousarray(np.tile(np.arange(16, dtype=f32)[None, :], (128, 1)))
    ii = np.arange(128)
    cumlt = (ii[:, None] < ii[None, :]).astype(BF)
    j8 = np.arange(8)
    u8t = (j8[:, None] < j8[None, :]).astype(BF)
    ec256 = np.ascontiguousarray(
        np.tile((np.arange(16, dtype=f32) * 256.0)[None, :], (128, 8)).reshape(128, 128))
    tok = (np.arange(8)[None, :] * 128 + ii[:, None]).astype(np.int32)
    tokab = np.ascontiguousarray(np.concatenate([tok, tok + 1024], axis=1))

    common = {
        "wqT": wqT, "wkT": wkT, "wvT": wvT, "wgT": wgT, "wmbT": wmbT,
        "bk": np.asarray(L0_attn_bk, f32), "bv": np.asarray(L0_attn_bv, f32),
        "bq": np.asarray(L0_attn_bq, f32), "bg": np.asarray(L0_attn_bg, f32),
        "bmb": np.asarray(L0_attn_bmb, f32),
        "mem_kh": mem_kh, "mem_hc": mem_hc, "norm_pc": norm_pc,
        "routerT": routerT,
        "gate_w": np.asarray(L0_ffn_gate_w, f32),
        "up_w": np.asarray(L0_ffn_up_w, f32),
        "down_w": np.asarray(L0_ffn_down_w, f32),
        "sgw": np.asarray(L0_ffn_shared_gate_w, f32),
        "suw": np.asarray(L0_ffn_shared_up_w, f32),
        "sdw": np.asarray(L0_ffn_shared_down_w, f32),
        "ident": ident, "identf": identf, "iota16": iota16,
        "cumlt": cumlt, "u8t": u8t, "ec256": ec256, "tokab": tokab,
    }
    in_maps = []
    for c in range(NCORES):
        m = dict(common)
        m["x"] = np.ascontiguousarray(x[c * T:(c + 1) * T])
        in_maps.append(m)

    key = "nc_dbg" if DEBUG else "nc"
    if key not in _NC_CACHE:
        _NC_CACHE[key] = build_program()
    nc = _NC_CACHE[key]
    res = run_bass_kernel_spmd(nc, in_maps, core_ids=list(range(NCORES)))
    _LAST_RES[0] = res
    out = np.concatenate([np.asarray(res.results[c]["out"]) for c in range(NCORES)],
                         axis=0)
    return out.reshape(1, 8192, D).astype(np.float32)
